# version 03
# baseline (speedup 1.0000x reference)
import numpy as np
import ml_dtypes
from contextlib import ExitStack

import concourse.bass as bass
import concourse.mybir as mybir
from concourse.bass_utils import run_bass_kernel_spmd

F32 = mybir.dt.float32
BF16 = mybir.dt.bfloat16
U8 = mybir.dt.uint8
AF = mybir.ActivationFunctionType
ALU = mybir.AluOpType
AX = mybir.AxisListType
BF = ml_dtypes.bfloat16

NC = 8
B, S, D = 2, 4096, 4096
T = B * S
TC = T // NC
KV = 1024
HEADS, KVH, HD = 32, 8, 128
NREP = HEADS // KVH
PAIRS_PER_CORE = (B * HEADS) // NC
NQC = S // 128
EPS = 1e-5
ISR = 1.0 / np.sqrt(128.0)
NEG = -3.0e38
NUM_REMAIN = 410



class Chain:

    def __init__(self):
        self.steps = []

    def dma(self, fn, q="sync"):
        if q == "gpsimd":
            self.steps.append((q, 0, 16, fn))
        else:
            self.steps.append((q, 16, 0, fn))

    def v(self, fn):
        self.steps.append(("vector", 1, 0, fn))

    def sc(self, fn):
        self.steps.append(("scalar", 1, 0, fn))

    def te(self, fn, inc=1):
        self.steps.append(("tensor", inc, 0, fn))

    def g(self, fn):
        self.steps.append(("gpsimd", 1, 0, fn))

    def emit(self, nc, block, sem, sem_gp):
        pres = []
        tot = tot_gp = 0
        for (_, inc, inc_gp, _) in self.steps:
            pres.append((tot, tot_gp))
            tot += inc
            tot_gp += inc_gp
        total, total_gp = tot, tot_gp
        steps = self.steps

        def emit_for(eng_name, e):
            last_m = last_g = -1
            for (name, inc, inc_gp, fn), (pre, pre_gp) in zip(steps, pres):
                if name != eng_name:
                    continue
                if pre > 0:
                    e.wait_ge(sem, pre)
                if pre_gp > 0:
                    e.wait_ge(sem_gp, pre_gp)
                inst = fn(e)
                if inc:
                    inst.then_inc(sem, inc)
                if inc_gp:
                    inst.then_inc(sem_gp, inc_gp)
            if eng_name == "sync":
                e.wait_ge(sem, total)
                if total_gp:
                    e.wait_ge(sem_gp, total_gp)

        @block.sync
        def _(e):
            emit_for("sync", e)

        @block.vector
        def _(e):
            emit_for("vector", e)

        @block.scalar
        def _(e):
            emit_for("scalar", e)

        @block.tensor
        def _(e):
            emit_for("tensor", e)

        @block.gpsimd
        def _(e):
            emit_for("gpsimd", e)


def _run(nc, in_maps):
    res = run_bass_kernel_spmd(nc, in_maps, core_ids=list(range(NC)))
    return res.results


def _bf(x):
    return np.ascontiguousarray(x, dtype=BF)


def _f32(x):
    return np.ascontiguousarray(x, dtype=np.float32)


IDENT_BF = np.eye(128, dtype=BF)
IDENT_F32 = np.eye(128, dtype=np.float32)



def build_qkv_launch(projs, bitnet, rope):
    nc = bass.Bass("TRN2", target_bir_lowering=False)
    x_in = nc.dram_tensor("x", [TC, D], F32, kind="ExternalInput").ap()
    wts = {}
    for (pname, odim) in projs:
        wts[pname] = {
            "WT": nc.dram_tensor(f"WT_{pname}", [D, odim], F32, kind="ExternalInput").ap(),
        }
        if bitnet:
            wts[pname]["Wraw"] = nc.dram_tensor(f"Wraw_{pname}", [odim, D], F32, kind="ExternalInput").ap()
            wts[pname]["scale"] = nc.dram_tensor(f"scale_{pname}", [128, odim // 128], F32, kind="ExternalInput").ap()
        wts[pname]["out"] = nc.dram_tensor(f"out_{pname}", [odim, TC], BF16, kind="ExternalOutput").ap()
    if rope:
        cos_in = nc.dram_tensor("cos_t", [128, TC], F32, kind="ExternalInput").ap()
        sin_in = nc.dram_tensor("sin_t", [128, TC], F32, kind="ExternalInput").ap()
    ident_in = nc.dram_tensor("identbf", [128, 128], BF16, kind="ExternalInput").ap()

    TB = 512
    NTB = TC // TB
    ch = Chain()
    with ExitStack() as ctx:
        sb = lambda n_, s, d: ctx.enter_context(nc.sbuf_tensor(n_, s, d))
        xt = sb("xt", [128, D], F32)
        xbf = sb("xbf", [128, D], BF16)
        xnT = sb("xnT", [128, 32 * TB], BF16)
        mu = sb("mu", [128, 1], F32)
        sq = sb("sq", [128, 1], F32)
        var = sb("var", [128, 1], F32)
        rstd = sb("rstd", [128, 1], F32)
        scr = sb("scr", [128, D], BF16)
        ident = sb("ident", [128, 128], BF16)
        wtile = sb("wtile", [128, 128 * 32], BF16)
        wf32 = sb("wf32", [128, 4096], F32)
        svec = sb("svec", [128, 64], F32)
        svtmp = sb("svtmp", [128, 1], F32)
        scl = sb("scl", [128, 64], F32)
        otile = sb("otile", [128, TB], F32)
        obf = sb("obf", [128, TB], BF16)
        cost = sb("cost", [128, TC], F32)
        sint = sb("sint", [128, TC], F32)
        rtmp = sb("rtmp", [128, TB], F32)
        rtmp2 = sb("rtmp2", [128, TB], F32)
        rsw = sb("rsw", [128, TB], F32)
        epsc = sb("epsc", [128, 1], F32)
        ptr = ctx.enter_context(nc.psum_tensor("ptr", [128, 128], BF16))
        pmm = ctx.enter_context(nc.psum_tensor("pmm", [128, TB], F32))
        sem = ctx.enter_context(nc.semaphore("chain"))
        sem_gp = ctx.enter_context(nc.semaphore("chain_gp"))
        block = ctx.enter_context(nc.Block())

        ch.dma(lambda e: e.dma_start(out=ident[:, :], in_=ident_in))
        ch.v(lambda e: e.memset(epsc[:, :], EPS))
        if rope:
            ch.dma(lambda e: e.dma_start(out=cost[:, :], in_=cos_in))
            ch.dma(lambda e: e.dma_start(out=sint[:, :], in_=sin_in))

        svec_cols = {}
        col = 0
        if bitnet:
            for (pname, odim) in projs:
                base = col
                svec_cols[pname] = base
                ch.dma(lambda e, w=wts[pname]["scale"], n_=odim // 128, base=base: e.dma_start(
                    out=scl[:, base:base + n_], in_=w))
                for jc in range(odim // 128):
                    cc = col
                    ch.dma(lambda e, w=wts[pname]["Wraw"], jc=jc: e.dma_start(
                        out=wf32[:, :], in_=w[jc * 128:(jc + 1) * 128, :]))
                    ch.v(lambda e, cc=cc: e.tensor_reduce(
                        out=svec[:, cc:cc + 1], in_=wf32[:, :], axis=AX.X,
                        op=ALU.add, apply_absolute_value=True))
                    ch.v(lambda e, cc=cc: e.tensor_scalar(
                        out=svtmp[:, :], in0=svec[:, cc:cc + 1], scalar1=1.0 / 4096.0,
                        scalar2=1e-8, op0=ALU.mult, op1=ALU.max))
                    ch.v(lambda e, cc=cc: e.tensor_tensor(
                        out=svec[:, cc:cc + 1], in0=svtmp[:, :], in1=scl[:, cc:cc + 1],
                        op=ALU.mult))
                    col += 1
        for tb in range(NTB):
            for tt in range(TB // 128):
                trow = tb * TB + tt * 128
                ch.dma(lambda e, trow=trow: e.dma_start(
                    out=xt[:, :], in_=x_in[trow:trow + 128, :]))
                ch.v(lambda e: e.tensor_reduce(out=mu[:, :], in_=xt[:, :], axis=AX.X, op=ALU.add))
                ch.v(lambda e: e.tensor_scalar(out=mu[:, :], in0=mu[:, :], scalar1=1.0 / D,
                                               scalar2=None, op0=ALU.mult))
                ch.v(lambda e: e.scalar_tensor_tensor(out=scr[:, :], in0=xt[:, :], scalar=1.0,
                                                      in1=xt[:, :], op0=ALU.mult, op1=ALU.mult,
                                                      accum_out=sq[:, :]))
                ch.v(lambda e: e.tensor_scalar(out=sq[:, :], in0=sq[:, :], scalar1=1.0 / D,
                                               scalar2=None, op0=ALU.mult))
                ch.v(lambda e: e.scalar_tensor_tensor(out=var[:, :], in0=mu[:, :], scalar=-1.0,
                                                      in1=mu[:, :], op0=ALU.mult, op1=ALU.mult))
                ch.v(lambda e: e.tensor_add(var[:, :], var[:, :], sq[:, :]))
                ch.sc(lambda e: e.activation(rstd[:, :], var[:, :], AF.Ln, bias=epsc[:, :]))
                ch.sc(lambda e: e.activation(rstd[:, :], rstd[:, :], AF.Exp, scale=-0.5))
                ch.v(lambda e: e.tensor_scalar(out=xbf[:, :], in0=xt[:, :], scalar1=mu[:, :],
                                               scalar2=rstd[:, :], op0=ALU.subtract, op1=ALU.mult))
                for c in range(32):
                    ch.te(lambda e, c=c: e.transpose(ptr[:, :], xbf[:, c * 128:(c + 1) * 128],
                                                     ident[:, :]))
                    ch.v(lambda e, c=c, tt=tt: e.tensor_copy(
                        out=xnT[:, c * TB + tt * 128: c * TB + tt * 128 + 128], in_=ptr[:, :]))
            for (pname, odim) in projs:
                wt_ap = wts[pname]["WT"]
                for jc in range(odim // 128):
                    ch.dma(lambda e, wt_ap=wt_ap, jc=jc: e.dma_start(
                        out=wtile[:, :],
                        in_=wt_ap.rearrange("(c p) j -> p c j", p=128)[:, :, jc * 128:(jc + 1) * 128]),
                        q="gpsimd")
                    if bitnet:
                        ch.sc(lambda e: e.activation(wtile[:, :], wtile[:, :], AF.Sign))
                    for dmc in range(32):
                        ch.te(lambda e, dmc=dmc: e.matmul(
                            pmm[:, :], wtile[:, dmc * 128:(dmc + 1) * 128],
                            xnT[:, dmc * TB:(dmc + 1) * TB],
                            start=(dmc == 0), stop=(dmc == 31)), inc=(1 if dmc == 31 else 0))
                    if bitnet:
                        cc = svec_cols[pname] + jc
                        ch.v(lambda e, cc=cc: e.tensor_scalar(
                            out=otile[:, :], in0=pmm[:, :], scalar1=svec[:, cc:cc + 1],
                            scalar2=None, op0=ALU.mult))
                    else:
                        ch.v(lambda e: e.tensor_copy(out=otile[:, :], in_=pmm[:, :]))
                    if rope and pname in ("q", "k"):
                        cs = cost[:, tb * TB:(tb + 1) * TB]
                        sn = sint[:, tb * TB:(tb + 1) * TB]
                        ch.v(lambda e: e.tensor_scalar(out=rsw[0:64, :], in0=otile[64:128, :],
                                                       scalar1=-1.0, scalar2=None, op0=ALU.mult))
                        ch.v(lambda e: e.tensor_copy(out=rsw[64:128, :], in_=otile[0:64, :]))
                        ch.v(lambda e, cs=cs: e.tensor_tensor(out=rtmp[:, :], in0=otile[:, :],
                                                              in1=cs, op=ALU.mult))
                        ch.v(lambda e, sn=sn: e.tensor_tensor(out=rtmp2[:, :], in0=rsw[:, :],
                                                              in1=sn, op=ALU.mult))
                        ch.v(lambda e: e.tensor_add(rtmp[:, :], rtmp[:, :], rtmp2[:, :]))
                        ch.v(lambda e: e.tensor_copy(out=obf[:, :], in_=rtmp[:, :]))
                    else:
                        ch.v(lambda e: e.tensor_copy(out=obf[:, :], in_=otile[:, :]))
                    ch.dma(lambda e, pname=pname, jc=jc, tb=tb: e.dma_start(
                        out=wts[pname]["out"][jc * 128:(jc + 1) * 128, tb * TB:(tb + 1) * TB],
                        in_=obf[:, :]))
        ch.emit(nc, block, sem, sem_gp)
    return nc


_QKV_CACHE = {}


def run_qkv(x_flat, weights, bitnet, rope, cos_half=None, sin_half=None):
    projs = tuple((n_, w["W"].shape[0]) for n_, w in weights.items())
    key = (projs, bitnet, rope)
    if key not in _QKV_CACHE:
        _QKV_CACHE[key] = build_qkv_launch(list(projs), bitnet, rope)
    nc = _QKV_CACHE[key]
    in_maps = []
    for c in range(NC):
        m = {"x": _f32(x_flat[c * TC:(c + 1) * TC]), "identbf": IDENT_BF}
        for n_, w in weights.items():
            m[f"WT_{n_}"] = _f32(np.ascontiguousarray(w["W"].T))
            if bitnet:
                m[f"Wraw_{n_}"] = _f32(w["W"])
                odim = w["W"].shape[0]
                m[f"scale_{n_}"] = _f32(np.ascontiguousarray(
                    w["scale"].reshape(odim // 128, 128).T))
        if rope:
            m["cos_t"] = _f32(cos_half[:, c * TC:(c + 1) * TC])
            m["sin_t"] = _f32(sin_half[:, c * TC:(c + 1) * TC])
        in_maps.append(m)
    res = _run(nc, in_maps)
    out = {}
    for n_, w in weights.items():
        odim = w["W"].shape[0]
        full = np.empty((odim, T), dtype=BF)
        for c in range(NC):
            full[:, c * TC:(c + 1) * TC] = res[c][f"out_{n_}"]
        out[n_] = full
    return out



def build_attn_launch(causal):
    nc = bass.Bass("TRN2", target_bir_lowering=False)
    PP = PAIRS_PER_CORE
    qh = nc.dram_tensor("qh", [PP * 128, S], BF16, kind="ExternalInput").ap()
    kh = nc.dram_tensor("kh", [PP * 128, S], BF16, kind="ExternalInput").ap()
    vT = nc.dram_tensor("vT", [PP * S, 128], BF16, kind="ExternalInput").ap()
    ao = nc.dram_tensor("ao", [PP * S, 128], BF16, kind="ExternalOutput").ap()
    ident_in = nc.dram_tensor("identbf", [128, 128], BF16, kind="ExternalInput").ap()
    if causal:
        cmask_in = nc.dram_tensor("cmask", [128, 4 * 512], F32, kind="ExternalInput").ap()

    NKB = S // 512
    ch = Chain()
    with ExitStack() as ctx:
        sb = lambda n_, s, d: ctx.enter_context(nc.sbuf_tensor(n_, s, d))
        qt = sb("qt", [128, S], BF16)
        kt = sb("kt", [128, S], BF16)
        vtt = sb("vtt", [128, 32 * 128], BF16)
        d_s = sb("d_s", [128, S], F32)
        p_s = sb("p_s", [128, S], BF16)
        ind = sb("ind", [128, S], BF16)
        pT = sb("pT", [128, 128], BF16)
        m_ = sb("m_", [128, 1], F32)
        mb = sb("mb", [128, 1], F32)
        lo = sb("lo", [128, 1], F32)
        hi = sb("hi", [128, 1], F32)
        mid = sb("mid", [128, 1], F32)
        hw = sb("hw", [128, 1], F32)
        cnt = sb("cnt", [128, 1], F32)
        seltk = ctx.enter_context(nc.sbuf_tensor("seltk", [128, 1], U8))
        ssum = sb("ssum", [128, 1], F32)
        rsum = sb("rsum", [128, 1], F32)
        otile = sb("otile", [128, 128], F32)
        obf = sb("obf", [128, 128], BF16)
        ident = sb("ident", [128, 128], BF16)
        cmask = sb("cmask_s", [128, 4 * 512], F32) if causal else None
        ptr = ctx.enter_context(nc.psum_tensor("ptr", [128, 128], BF16))
        pqk = ctx.enter_context(nc.psum_tensor("pqk", [128, 512], F32))
        pov = ctx.enter_context(nc.psum_tensor("pov", [128, 128], F32))
        sem = ctx.enter_context(nc.semaphore("chain"))
        sem_gp = ctx.enter_context(nc.semaphore("chain_gp"))
        block = ctx.enter_context(nc.Block())

        ch.dma(lambda e: e.dma_start(out=ident[:, :], in_=ident_in))
        if causal:
            ch.dma(lambda e: e.dma_start(out=cmask[:, :], in_=cmask_in))

        for pr in range(PP):
            ch.dma(lambda e, pr=pr: e.dma_start(out=qt[:, :], in_=qh[pr * 128:(pr + 1) * 128, :]))
            ch.dma(lambda e, pr=pr: e.dma_start(out=kt[:, :], in_=kh[pr * 128:(pr + 1) * 128, :]))
            ch.dma(lambda e, pr=pr: e.dma_start(
                out=vtt[:, :],
                in_=vT.rearrange("(pp c p) d -> p (pp c) d", p=128, pp=PP)[:, pr * 32:(pr + 1) * 32, :]),
                q="gpsimd")
            for qc in range(NQC):
                nkb = (qc * 128 // 512 + 1) if causal else NKB
                kwid = nkb * 512
                for kb in range(nkb):
                    ch.te(lambda e, qc=qc, kb=kb: e.matmul(
                        pqk[:, :], qt[:, qc * 128:(qc + 1) * 128],
                        kt[:, kb * 512:(kb + 1) * 512], start=True, stop=True))
                    ch.v(lambda e, kb=kb: e.tensor_copy(out=d_s[:, kb * 512:(kb + 1) * 512],
                                                        in_=pqk[:, :]))
                if causal:
                    ch.v(lambda e, kwid=kwid: e.tensor_reduce(
                        out=lo[:, :], in_=d_s[:, 0:kwid], axis=AX.X, op=ALU.min))
                    off = (qc % 4) * 512
                    dg = (nkb - 1) * 512
                    ch.v(lambda e, off=off, dg=dg: e.tensor_tensor(
                        out=d_s[:, dg:dg + 512], in0=d_s[:, dg:dg + 512],
                        in1=cmask[:, off:off + 512], op=ALU.add))
                ch.v(lambda e, kwid=kwid: e.tensor_reduce(
                    out=m_[:, :], in_=d_s[:, 0:kwid], axis=AX.X, op=ALU.max))
                if causal:
                    ch.v(lambda e: e.scalar_tensor_tensor(
                        out=mid[:, :], in0=lo[:, :], scalar=1.0, in1=m_[:, :],
                        op0=ALU.mult, op1=ALU.add))
                    ch.v(lambda e: e.tensor_scalar(
                        out=mid[:, :], in0=mid[:, :], scalar1=0.5, scalar2=None, op0=ALU.mult))
                    ch.v(lambda e: e.scalar_tensor_tensor(
                        out=hw[:, :], in0=lo[:, :], scalar=-1.0, in1=m_[:, :],
                        op0=ALU.mult, op1=ALU.add))
                    ch.v(lambda e: e.tensor_scalar(
                        out=hw[:, :], in0=hw[:, :], scalar1=0.5, scalar2=None, op0=ALU.mult))
                    for it in range(20):
                        ch.v(lambda e, kwid=kwid: e.tensor_scalar(
                            out=ind[:, 0:kwid], in0=d_s[:, 0:kwid], scalar1=mid[:, :],
                            scalar2=None, op0=ALU.is_ge, op1=ALU.add, accum_out=cnt[:, :]))
                        ch.v(lambda e: e.tensor_scalar(
                            out=hw[:, :], in0=hw[:, :], scalar1=0.5, scalar2=None, op0=ALU.mult))
                        ch.v(lambda e: e.tensor_scalar(
                            out=cnt[:, :], in0=cnt[:, :], scalar1=float(NUM_REMAIN),
                            scalar2=2.0, op0=ALU.is_ge, op1=ALU.mult))
                        ch.v(lambda e: e.scalar_tensor_tensor(
                            out=cnt[:, :], in0=cnt[:, :], scalar=-1.0, in1=hw[:, :],
                            op0=ALU.add, op1=ALU.mult))
                        ch.v(lambda e: e.tensor_add(mid[:, :], mid[:, :], cnt[:, :]))
                    ch.v(lambda e: e.scalar_tensor_tensor(
                        out=mid[:, :], in0=hw[:, :], scalar=-4.0, in1=mid[:, :],
                        op0=ALU.mult, op1=ALU.add))
                    ch.v(lambda e, kwid=kwid: e.tensor_scalar(
                        out=ind[:, 0:kwid], in0=d_s[:, 0:kwid], scalar1=mid[:, :],
                        scalar2=None, op0=ALU.is_ge))
                ch.v(lambda e: e.tensor_scalar(
                    out=mb[:, :], in0=m_[:, :], scalar1=-ISR, scalar2=None, op0=ALU.mult))
                if causal:
                    ch.sc(lambda e, kwid=kwid: e.activation(
                        p_s[:, 0:kwid], d_s[:, 0:kwid], AF.Exp, bias=mb[:, :], scale=ISR))
                    ch.v(lambda e, kwid=kwid: e.scalar_tensor_tensor(
                        out=p_s[:, 0:kwid], in0=p_s[:, 0:kwid], scalar=1.0,
                        in1=ind[:, 0:kwid], op0=ALU.mult, op1=ALU.mult,
                        accum_out=ssum[:, :]))
                else:
                    ch.sc(lambda e, kwid=kwid: e.activation(
                        p_s[:, 0:kwid], d_s[:, 0:kwid], AF.Exp, bias=mb[:, :], scale=ISR,
                        accum_out=ssum[:, :]))
                ch.v(lambda e: e.reciprocal(rsum[:, :], ssum[:, :]))
                nkc = kwid // 128
                for kc in range(nkc):
                    ch.te(lambda e, kc=kc: e.transpose(
                        ptr[:, :], p_s[:, kc * 128:(kc + 1) * 128], ident[:, :]))
                    ch.v(lambda e: e.tensor_copy(out=pT[:, :], in_=ptr[:, :]))
                    ch.te(lambda e, kc=kc, first=(kc == 0), last=(kc == nkc - 1): e.matmul(
                        pov[:, :], pT[:, :], vtt[:, kc * 128:(kc + 1) * 128],
                        start=first, stop=last), inc=1)
                ch.v(lambda e: e.tensor_scalar(
                    out=obf[:, :], in0=pov[:, :], scalar1=rsum[:, :], scalar2=None,
                    op0=ALU.mult))
                ch.dma(lambda e, pr=pr, qc=qc: e.dma_start(
                    out=ao[pr * S + qc * 128: pr * S + (qc + 1) * 128, :], in_=obf[:, :]))
        ch.emit(nc, block, sem, sem_gp)
    return nc


_ATTN_CACHE = {}


CMASK = None

def _cmask():
    global CMASK
    if CMASK is None:
        m = np.zeros((128, 4, 512), np.float32)
        for off in range(4):
            p = np.arange(128)[:, None]
            j = np.arange(512)[None, :]
            m[:, off, :] = np.where(p + off * 128 - j >= 0, 0.0, NEG)
        CMASK = np.ascontiguousarray(m.reshape(128, 2048))
    return CMASK


def run_attn(qh, kh, vth, causal):
    if causal not in _ATTN_CACHE:
        _ATTN_CACHE[causal] = build_attn_launch(causal)
    nc = _ATTN_CACHE[causal]
    PP = PAIRS_PER_CORE
    in_maps = []
    for c in range(NC):
        in_maps.append({
            "qh": qh[c * PP * 128:(c + 1) * PP * 128],
            "kh": kh[c * PP * 128:(c + 1) * PP * 128],
            "vT": vth[c * PP * S:(c + 1) * PP * S],
            "identbf": IDENT_BF,
        })
        if causal:
            in_maps[-1]["cmask"] = _cmask()
    res = _run(nc, in_maps)
    ao = np.empty((B * HEADS * S, 128), dtype=BF)
    for c in range(NC):
        ao[c * PP * S:(c + 1) * PP * S] = res[c]["ao"]
    return ao



def emit_ln_stats(ch, src, mu, sq, var, rstd, scr, epsc, width):
    ch.v(lambda e: e.tensor_reduce(out=mu[:, :], in_=src, axis=AX.X, op=ALU.add))
    ch.v(lambda e: e.tensor_scalar(out=mu[:, :], in0=mu[:, :], scalar1=1.0 / width,
                                   scalar2=None, op0=ALU.mult))
    ch.v(lambda e: e.scalar_tensor_tensor(out=scr, in0=src, scalar=1.0,
                                          in1=src, op0=ALU.mult, op1=ALU.mult,
                                          accum_out=sq[:, :]))
    ch.v(lambda e: e.tensor_scalar(out=sq[:, :], in0=sq[:, :], scalar1=1.0 / width,
                                   scalar2=None, op0=ALU.mult))
    ch.v(lambda e: e.scalar_tensor_tensor(out=var[:, :], in0=mu[:, :], scalar=-1.0,
                                          in1=mu[:, :], op0=ALU.mult, op1=ALU.mult))
    ch.v(lambda e: e.tensor_add(var[:, :], var[:, :], sq[:, :]))
    ch.sc(lambda e: e.activation(rstd[:, :], var[:, :], AF.Ln, bias=epsc[:, :]))
    ch.sc(lambda e: e.activation(rstd[:, :], rstd[:, :], AF.Exp, scale=-0.5))



def build_out_launch(bitnet):
    nc = bass.Bass("TRN2", target_bir_lowering=False)
    aT_in = nc.dram_tensor("aT", [D, TC], BF16, kind="ExternalInput").ap()
    wt_in = nc.dram_tensor("WT", [D, D], F32, kind="ExternalInput").ap()
    if bitnet:
        wraw_in = nc.dram_tensor("Wraw", [D, D], F32, kind="ExternalInput").ap()
        scale_in = nc.dram_tensor("scale", [128, 32], F32, kind="ExternalInput").ap()
    x_in = nc.dram_tensor("x", [TC, D], F32, kind="ExternalInput").ap()
    xo_out = nc.dram_tensor("xo", [TC, D], F32, kind="ExternalOutput").ap()
    identf_in = nc.dram_tensor("identf", [128, 128], F32, kind="ExternalInput").ap()

    HALF = 512
    ch = Chain()
    with ExitStack() as ctx:
        sb = lambda n_, s, d: ctx.enter_context(nc.sbuf_tensor(n_, s, d))
        aTs = sb("aTs", [128, 32 * HALF], BF16)
        wtile = sb("wtile", [128, 128 * 32], BF16)
        wf32 = sb("wf32", [128, 4096], F32)
        svec = sb("svec", [128, 32], F32)
        svtmp = sb("svtmp", [128, 1], F32)
        scl = sb("scl", [128, 32], F32)
        ftile = sb("ftile", [128, 128], F32)
        ftok = sb("ftok", [128, 4 * D], F32)
        xt = sb("xt", [128, D], F32)
        scr = sb("scr", [128, D], BF16)
        mu = sb("mu", [128, 1], F32)
        sq = sb("sq", [128, 1], F32)
        var = sb("var", [128, 1], F32)
        rstd = sb("rstd", [128, 1], F32)
        epsc = sb("epsc", [128, 1], F32)
        identf = sb("identf_s", [128, 128], F32)
        ptrf = ctx.enter_context(nc.psum_tensor("ptrf", [128, 128], F32))
        pmm = ctx.enter_context(nc.psum_tensor("pmm", [128, 128], F32))
        sem = ctx.enter_context(nc.semaphore("chain"))
        sem_gp = ctx.enter_context(nc.semaphore("chain_gp"))
        block = ctx.enter_context(nc.Block())

        ch.dma(lambda e: e.dma_start(out=identf[:, :], in_=identf_in))
        ch.v(lambda e: e.memset(epsc[:, :], EPS))
        if bitnet:
            ch.dma(lambda e: e.dma_start(out=scl[:, :], in_=scale_in))
            for jc in range(32):
                ch.dma(lambda e, jc=jc: e.dma_start(
                    out=wf32[:, :], in_=wraw_in[jc * 128:(jc + 1) * 128, :]))
                ch.v(lambda e, jc=jc: e.tensor_reduce(
                    out=svec[:, jc:jc + 1], in_=wf32[:, :], axis=AX.X,
                    op=ALU.add, apply_absolute_value=True))
                ch.v(lambda e, jc=jc: e.tensor_scalar(
                    out=svtmp[:, :], in0=svec[:, jc:jc + 1], scalar1=1.0 / 4096.0,
                    scalar2=1e-8, op0=ALU.mult, op1=ALU.max))
                ch.v(lambda e, jc=jc: e.tensor_tensor(
                    out=svec[:, jc:jc + 1], in0=svtmp[:, :], in1=scl[:, jc:jc + 1],
                    op=ALU.mult))
        for half in range(TC // HALF):
            ch.dma(lambda e, half=half: e.dma_start(
                out=aTs[:, :],
                in_=aT_in.rearrange("(c p) t -> p c t", p=128)[:, :, half * HALF:(half + 1) * HALF]))
            for jc in range(32):
                ch.dma(lambda e, jc=jc: e.dma_start(
                    out=wtile[:, :],
                    in_=wt_in.rearrange("(c p) j -> p c j", p=128)[:, :, jc * 128:(jc + 1) * 128]),
                    q="gpsimd")
                if bitnet:
                    ch.sc(lambda e: e.activation(wtile[:, :], wtile[:, :], AF.Sign))
                for tt in range(HALF // 128):
                    for dc in range(32):
                        ch.te(lambda e, dc=dc, tt=tt: e.matmul(
                            pmm[:, :], wtile[:, dc * 128:(dc + 1) * 128],
                            aTs[:, dc * HALF + tt * 128: dc * HALF + tt * 128 + 128],
                            start=(dc == 0), stop=(dc == 31)), inc=(1 if dc == 31 else 0))
                    if bitnet:
                        ch.v(lambda e, jc=jc: e.tensor_scalar(
                            out=ftile[:, :], in0=pmm[:, :], scalar1=svec[:, jc:jc + 1],
                            scalar2=None, op0=ALU.mult))
                    else:
                        ch.v(lambda e: e.tensor_copy(out=ftile[:, :], in_=pmm[:, :]))
                    ch.te(lambda e: e.transpose(ptrf[:, :], ftile[:, :], identf[:, :]))
                    ch.v(lambda e, tt=tt, jc=jc: e.tensor_copy(
                        out=ftok[:, tt * D + jc * 128: tt * D + jc * 128 + 128],
                        in_=ptrf[:, :]))
            for tt in range(HALF // 128):
                trow = half * HALF + tt * 128
                fv = ftok[:, tt * D:(tt + 1) * D]
                emit_ln_stats(ch, fv, mu, sq, var, rstd, scr[:, :], epsc, D)
                ch.dma(lambda e, trow=trow: e.dma_start(
                    out=xt[:, :], in_=x_in[trow:trow + 128, :]))
                ch.v(lambda e, fv=fv: e.tensor_scalar(
                    out=fv, in0=fv, scalar1=mu[:, :], scalar2=rstd[:, :],
                    op0=ALU.subtract, op1=ALU.mult))
                ch.v(lambda e, fv=fv: e.tensor_add(xt[:, :], xt[:, :], fv))
                ch.dma(lambda e, trow=trow: e.dma_start(
                    out=xo_out[trow:trow + 128, :], in_=xt[:, :]))
        ch.emit(nc, block, sem, sem_gp)
    return nc


_OUT_CACHE = {}


def run_out(aT, W, scale, x_flat, bitnet):
    if bitnet not in _OUT_CACHE:
        _OUT_CACHE[bitnet] = build_out_launch(bitnet)
    nc = _OUT_CACHE[bitnet]
    WT = _f32(np.ascontiguousarray(W.T))
    if bitnet:
        scl = _f32(np.ascontiguousarray(scale.reshape(32, 128).T))
    in_maps = []
    for c in range(NC):
        m = {
            "aT": np.ascontiguousarray(aT[:, c * TC:(c + 1) * TC]),
            "WT": WT,
            "x": _f32(x_flat[c * TC:(c + 1) * TC]),
            "identf": IDENT_F32,
        }
        if bitnet:
            m["Wraw"] = _f32(W)
            m["scale"] = scl
        in_maps.append(m)
    res = _run(nc, in_maps)
    xo = np.empty((T, D), dtype=np.float32)
    for c in range(NC):
        xo[c * TC:(c + 1) * TC] = res[c]["xo"]
    return xo



def build_ffn_launch(variant):
    nc = bass.Bass("TRN2", target_bir_lowering=False)
    x_in = nc.dram_tensor("x", [TC, D], F32, kind="ExternalInput").ap()
    wt_in = nc.dram_tensor("WT", [D, D], F32, kind="ExternalInput").ap()
    bias_in = nc.dram_tensor("bias", [128, 32], F32, kind="ExternalInput").ap()
    NER = 256 if variant == "l0" else 128
    at_in = nc.dram_tensor("AT", [D, NER], F32, kind="ExternalInput").ap()
    b2_in = nc.dram_tensor("B2", [NER, D], F32, kind="ExternalInput").ap()
    if variant == "l0":
        coeffs_in = nc.dram_tensor("coeffs", [16, 1], F32, kind="ExternalInput").ap()
        scales_in = nc.dram_tensor("scales", [16, 1], F32, kind="ExternalInput").ap()
        m1_in = nc.dram_tensor("M1", [16, 256], F32, kind="ExternalInput").ap()
    else:
        gwt_in = nc.dram_tensor("gwT", [D, 8], F32, kind="ExternalInput").ap()
        m2_in = nc.dram_tensor("M2", [8, 128], F32, kind="ExternalInput").ap()
    identb_in = nc.dram_tensor("identbf", [128, 128], BF16, kind="ExternalInput").ap()
    identf_in = nc.dram_tensor("identf", [128, 128], F32, kind="ExternalInput").ap()
    xo_out = nc.dram_tensor("xo", [TC, D], F32, kind="ExternalOutput").ap()

    HALF = 512
    NERC = NER // 128
    ch = Chain()
    with ExitStack() as ctx:
        sb = lambda n_, s, d: ctx.enter_context(nc.sbuf_tensor(n_, s, d))
        xt = sb("xt", [128, D], F32)
        xbf = sb("xbf", [128, D], BF16)
        xnT = sb("xnT", [128, 32 * HALF], BF16)
        mu = sb("mu", [128, 1], F32)
        sq = sb("sq", [128, 1], F32)
        var = sb("var", [128, 1], F32)
        rstd = sb("rstd", [128, 1], F32)
        scr = sb("scr", [128, D], BF16)
        identb = sb("identb_s", [128, 128], BF16)
        identf = sb("identf_s", [128, 128], F32)
        epsc = sb("epsc", [128, 1], F32)
        wtile = sb("wtile", [128, 128 * 32], BF16)
        biasv = sb("biasv", [128, 32], F32)
        at_s = [sb(f"at{i}", [128, 32 * 128], BF16) for i in range(NERC)]
        b2_s = [sb(f"b2{i}", [128, D], BF16) for i in range(NERC)]
        xa_s = sb("xa_s", [128, NERC * HALF], BF16)
        ftile = sb("ftile", [128, HALF], F32)
        ftok = sb("ftok", [128, 4 * D], F32)
        if variant == "l0":
            c16 = sb("c16", [16, 1], F32)
            s16 = sb("s16", [16, 1], F32)
            m1_s = sb("m1_s", [16, 256], F32)
            wex = sb("wex", [128, NERC], F32)
        else:
            gw_s = sb("gw_s", [128, 32 * 8], BF16)
            m2_s = sb("m2_s", [8, 128], F32)
            gl = sb("gl", [128, 8], F32)
            top8 = sb("top8", [128, 8], F32)
            ed = sb("ed", [128, 1], F32)
            ee = sb("ee", [128, 1], F32)
            w1 = sb("w1", [128, 1], F32)
            w2 = sb("w2", [128, 1], F32)
            wfull = sb("wfull", [128, 8], F32)
            wf2 = sb("wf2", [128, 8], F32)
            wT_s = sb("wT_s", [8, 128], F32)
            we_s = sb("we_s", [128, 4 * 128], F32)
        ptrb = ctx.enter_context(nc.psum_tensor("ptrb", [128, 128], BF16))
        ptrf = ctx.enter_context(nc.psum_tensor("ptrf", [128, 128], F32))
        pmm = ctx.enter_context(nc.psum_tensor("pmm", [128, HALF], F32))
        pxa = ctx.enter_context(nc.psum_tensor("pxa", [128, HALF], F32))
        paux = ctx.enter_context(nc.psum_tensor("paux", [128, 128], F32))
        sem = ctx.enter_context(nc.semaphore("chain"))
        sem_gp = ctx.enter_context(nc.semaphore("chain_gp"))
        block = ctx.enter_context(nc.Block())

        ch.dma(lambda e: e.dma_start(out=identb[:, :], in_=identb_in))
        ch.dma(lambda e: e.dma_start(out=identf[:, :], in_=identf_in))
        ch.dma(lambda e: e.dma_start(out=biasv[:, :], in_=bias_in))
        ch.v(lambda e: e.memset(epsc[:, :], EPS))
        for erc in range(NERC):
            ch.dma(lambda e, erc=erc: e.dma_start(
                out=at_s[erc][:, :],
                in_=at_in.rearrange("(c p) j -> p c j", p=128)[:, :, erc * 128:(erc + 1) * 128]),
                q="gpsimd")
            ch.dma(lambda e, erc=erc: e.dma_start(
                out=b2_s[erc][:, :], in_=b2_in[erc * 128:(erc + 1) * 128, :]), q="gpsimd")
        if variant == "l0":
            ch.dma(lambda e: e.dma_start(out=c16[:, :], in_=coeffs_in))
            ch.dma(lambda e: e.dma_start(out=s16[:, :], in_=scales_in))
            ch.dma(lambda e: e.dma_start(out=m1_s[:, :], in_=m1_in))
            ch.v(lambda e: e.tensor_tensor(out=c16[:, :], in0=c16[:, :], in1=s16[:, :],
                                           op=ALU.mult))
            for erc in range(NERC):
                ch.te(lambda e, erc=erc: e.matmul(
                    paux[:, 0:1], m1_s[:, erc * 128:(erc + 1) * 128], c16[:, :],
                    start=True, stop=True))
                ch.v(lambda e, erc=erc: e.tensor_copy(out=wex[:, erc:erc + 1],
                                                      in_=paux[:, 0:1]))
        else:
            ch.dma(lambda e: e.dma_start(
                out=gw_s[:, :], in_=gwt_in.rearrange("(c p) j -> p c j", p=128)),
                q="gpsimd")
            ch.dma(lambda e: e.dma_start(out=m2_s[:, :], in_=m2_in))

        for half in range(TC // HALF):
            for tt in range(HALF // 128):
                trow = half * HALF + tt * 128
                ch.dma(lambda e, trow=trow: e.dma_start(
                    out=xt[:, :], in_=x_in[trow:trow + 128, :]))
                emit_ln_stats(ch, xt[:, :], mu, sq, var, rstd, scr[:, :], epsc, D)
                ch.v(lambda e: e.tensor_scalar(out=xbf[:, :], in0=xt[:, :], scalar1=mu[:, :],
                                               scalar2=rstd[:, :], op0=ALU.subtract, op1=ALU.mult))
                for c in range(32):
                    ch.te(lambda e, c=c: e.transpose(ptrb[:, :], xbf[:, c * 128:(c + 1) * 128],
                                                     identb[:, :]))
                    ch.v(lambda e, c=c, tt=tt: e.tensor_copy(
                        out=xnT[:, c * HALF + tt * 128: c * HALF + tt * 128 + 128],
                        in_=ptrb[:, :]))
            if variant == "l1":
                for tt in range(HALF // 128):
                    for dc in range(32):
                        ch.te(lambda e, dc=dc, tt=tt: e.matmul(
                            paux[0:128, 0:8], xnT[:, dc * HALF + tt * 128: dc * HALF + tt * 128 + 128],
                            gw_s[:, dc * 8:(dc + 1) * 8],
                            start=(dc == 0), stop=(dc == 31)), inc=(1 if dc == 31 else 0))
                    ch.v(lambda e: e.tensor_copy(out=gl[:, :], in_=paux[0:128, 0:8]))
                    ch.v(lambda e: e.max(out=top8[:, :], in_=gl[:, :]))
                    ch.v(lambda e: e.tensor_sub(ed[:, :], top8[:, 1:2], top8[:, 0:1]))
                    ch.sc(lambda e: e.activation(ee[:, :], ed[:, :], AF.Exp))
                    ch.v(lambda e: e.tensor_scalar(out=w1[:, :], in0=ee[:, :], scalar1=1.0,
                                                   scalar2=None, op0=ALU.add))
                    ch.v(lambda e: e.reciprocal(w1[:, :], w1[:, :]))
                    ch.v(lambda e: e.tensor_mul(w2[:, :], ee[:, :], w1[:, :]))
                    ch.v(lambda e: e.tensor_scalar(out=wfull[:, :], in0=gl[:, :],
                                                   scalar1=top8[:, 0:1], scalar2=w1[:, :],
                                                   op0=ALU.is_equal, op1=ALU.mult))
                    ch.v(lambda e: e.tensor_scalar(out=wf2[:, :], in0=gl[:, :],
                                                   scalar1=top8[:, 1:2], scalar2=w2[:, :],
                                                   op0=ALU.is_equal, op1=ALU.mult))
                    ch.v(lambda e: e.tensor_add(wfull[:, :], wfull[:, :], wf2[:, :]))
                    ch.te(lambda e: e.transpose(paux[0:8, 0:128], wfull[:, :], identf[:, :]))
                    ch.v(lambda e: e.tensor_copy(out=wT_s[:, :], in_=paux[0:8, 0:128]))
                    ch.te(lambda e: e.matmul(paux[0:128, 0:128], m2_s[:, :], wT_s[:, :],
                                             start=True, stop=True))
                    ch.v(lambda e, tt=tt: e.tensor_copy(
                        out=we_s[:, tt * 128:(tt + 1) * 128], in_=paux[0:128, 0:128]))
            for erc in range(NERC):
                for dc in range(32):
                    ch.te(lambda e, erc=erc, dc=dc: e.matmul(
                        pxa[:, :], at_s[erc][:, dc * 128:(dc + 1) * 128],
                        xnT[:, dc * HALF:(dc + 1) * HALF],
                        start=(dc == 0), stop=(dc == 31)), inc=(1 if dc == 31 else 0))
                if variant == "l0":
                    ch.v(lambda e, erc=erc: e.tensor_scalar(
                        out=xa_s[:, erc * HALF:(erc + 1) * HALF], in0=pxa[:, :],
                        scalar1=wex[:, erc:erc + 1], scalar2=None, op0=ALU.mult))
                else:
                    for tt in range(HALF // 128):
                        ch.v(lambda e, erc=erc, tt=tt: e.tensor_tensor(
                            out=xa_s[:, erc * HALF + tt * 128: erc * HALF + tt * 128 + 128],
                            in0=pxa[:, tt * 128:(tt + 1) * 128],
                            in1=we_s[:, tt * 128:(tt + 1) * 128], op=ALU.mult))
            for jc in range(32):
                ch.dma(lambda e, jc=jc: e.dma_start(
                    out=wtile[:, :],
                    in_=wt_in.rearrange("(c p) j -> p c j", p=128)[:, :, jc * 128:(jc + 1) * 128]),
                    q="gpsimd")
                for dc in range(32):
                    ch.te(lambda e, dc=dc: e.matmul(
                        pmm[:, :], wtile[:, dc * 128:(dc + 1) * 128],
                        xnT[:, dc * HALF:(dc + 1) * HALF],
                        start=(dc == 0), stop=False), inc=0)
                for erc in range(NERC):
                    ch.te(lambda e, erc=erc, jc=jc: e.matmul(
                        pmm[:, :], b2_s[erc][:, jc * 128:(jc + 1) * 128],
                        xa_s[:, erc * HALF:(erc + 1) * HALF],
                        start=False, stop=(erc == NERC - 1)), inc=(1 if erc == NERC - 1 else 0))
                ch.v(lambda e, jc=jc: e.tensor_scalar(
                    out=ftile[:, :], in0=pmm[:, :], scalar1=biasv[:, jc:jc + 1],
                    scalar2=None, op0=ALU.add))
                for tt in range(HALF // 128):
                    ch.te(lambda e, tt=tt: e.transpose(
                        ptrf[:, :], ftile[:, tt * 128:(tt + 1) * 128], identf[:, :]))
                    ch.v(lambda e, tt=tt, jc=jc: e.tensor_copy(
                        out=ftok[:, tt * D + jc * 128: tt * D + jc * 128 + 128],
                        in_=ptrf[:, :]))
            for tt in range(HALF // 128):
                trow = half * HALF + tt * 128
                fv = ftok[:, tt * D:(tt + 1) * D]
                emit_ln_stats(ch, fv, mu, sq, var, rstd, scr[:, :], epsc, D)
                ch.dma(lambda e, trow=trow: e.dma_start(
                    out=xt[:, :], in_=x_in[trow:trow + 128, :]))
                ch.v(lambda e, fv=fv: e.tensor_scalar(
                    out=fv, in0=fv, scalar1=mu[:, :], scalar2=rstd[:, :],
                    op0=ALU.subtract, op1=ALU.mult))
                ch.v(lambda e, fv=fv: e.tensor_add(xt[:, :], xt[:, :], fv))
                ch.dma(lambda e, trow=trow: e.dma_start(
                    out=xo_out[trow:trow + 128, :], in_=xt[:, :]))
        ch.emit(nc, block, sem, sem_gp)
    return nc


_FFN_CACHE = {}


def run_ffn_l0(x_flat, W, bias, A, Bf, coeffs, scales):
    if "l0" not in _FFN_CACHE:
        _FFN_CACHE["l0"] = build_ffn_launch("l0")
    nc = _FFN_CACHE["l0"]
    WT = _f32(np.ascontiguousarray(W.T))
    AT = _f32(np.ascontiguousarray(A.reshape(256, D).T))
    B2 = _f32(np.ascontiguousarray(Bf.transpose(0, 2, 1).reshape(256, D)))
    biasm = _f32(np.ascontiguousarray(bias.reshape(32, 128).T))
    M1 = np.zeros((16, 256), np.float32)
    for e_ in range(16):
        M1[e_, e_ * 16:(e_ + 1) * 16] = 1.0
    in_maps = []
    for c in range(NC):
        in_maps.append({
            "x": _f32(x_flat[c * TC:(c + 1) * TC]), "WT": WT, "bias": biasm,
            "AT": AT, "B2": B2, "coeffs": _f32(coeffs.reshape(16, 1)),
            "scales": _f32(scales.reshape(16, 1)), "M1": M1,
            "identbf": IDENT_BF, "identf": IDENT_F32,
        })
    res = _run(nc, in_maps)
    xo = np.empty((T, D), dtype=np.float32)
    for c in range(NC):
        xo[c * TC:(c + 1) * TC] = res[c]["xo"]
    return xo


def run_ffn_l1(x_flat, W, bias, lora_A, lora_B, gate_w):
    if "l1" not in _FFN_CACHE:
        _FFN_CACHE["l1"] = build_ffn_launch("l1")
    nc = _FFN_CACHE["l1"]
    WT = _f32(np.ascontiguousarray(W.T))
    AT = _f32(np.ascontiguousarray(lora_A.reshape(128, D).T))
    B2 = _f32(np.ascontiguousarray(lora_B.transpose(0, 2, 1).reshape(128, D)))
    biasm = _f32(np.ascontiguousarray(bias.reshape(32, 128).T))
    gwT = _f32(np.ascontiguousarray(gate_w.T))
    M2 = np.zeros((8, 128), np.float32)
    for e_ in range(8):
        M2[e_, e_ * 16:(e_ + 1) * 16] = 2.0
    in_maps = []
    for c in range(NC):
        in_maps.append({
            "x": _f32(x_flat[c * TC:(c + 1) * TC]), "WT": WT, "bias": biasm,
            "AT": AT, "B2": B2, "gwT": gwT, "M2": M2,
            "identbf": IDENT_BF, "identf": IDENT_F32,
        })
    res = _run(nc, in_maps)
    xo = np.empty((T, D), dtype=np.float32)
    for c in range(NC):
        xo[c * TC:(c + 1) * TC] = res[c]["xo"]
    return xo



def _heads_q(q0):
    return np.ascontiguousarray(
        q0.reshape(HEADS, 128, B, S).transpose(2, 0, 1, 3).reshape(B * HEADS * 128, S))


def _heads_kv(k0):
    r = k0.reshape(KVH, 128, B, S).transpose(2, 0, 1, 3)
    r = np.repeat(r, NREP, axis=1)
    return np.ascontiguousarray(r.reshape(B * HEADS * 128, S))


def _heads_vT(v0):
    r = v0.reshape(KVH, 128, B, S).transpose(2, 0, 3, 1)
    r = np.repeat(r, NREP, axis=1)
    return np.ascontiguousarray(r.reshape(B * HEADS * S, 128))


def _gather_heads(ao):
    a = ao.reshape(B, HEADS, S, 128).transpose(1, 3, 0, 2)
    return np.ascontiguousarray(a.reshape(D, T))


_ROPE = None


def _rope_tables():
    global _ROPE
    if _ROPE is None:
        i = np.arange(0, 128, 2, dtype=np.float64) / 128.0
        inv32 = (1.0 / (10000.0 ** i)).astype(np.float32)
        t32 = np.arange(S, dtype=np.float32)
        freqs32 = t32[:, None] * inv32[None, :]
        cos = np.cos(freqs32.astype(np.float64)).astype(np.float32)
        sin = np.sin(freqs32.astype(np.float64)).astype(np.float32)
        cos_t = np.concatenate([cos, cos], axis=1).T
        sin_t = np.concatenate([sin, sin], axis=1).T
        cos_full = np.tile(cos_t, (1, B))
        sin_full = np.tile(sin_t, (1, B))
        _ROPE = (np.ascontiguousarray(cos_full), np.ascontiguousarray(sin_full))
    return _ROPE


def kernel(L0_attn_hidden_states, L0_attn_q_weight, L0_attn_q_scale,
           L0_attn_k_weight, L0_attn_k_scale, L0_attn_v_weight, L0_attn_v_scale,
           L0_attn_o_weight, L0_attn_o_scale, L0_ffn_W, L0_ffn_bias, L0_ffn_A,
           L0_ffn_Bf, L0_ffn_coeffs, L0_ffn_scales, L1_attn_Wq, L1_attn_Wk,
           L1_attn_Wv, L1_attn_Wo, L1_ffn_gate_w, L1_ffn_base_w, L1_ffn_base_b,
           L1_ffn_lora_A, L1_ffn_lora_B):
    x = _f32(np.asarray(L0_attn_hidden_states).reshape(T, D))

    pr = run_qkv(x, {
        "q": {"W": np.asarray(L0_attn_q_weight), "scale": np.asarray(L0_attn_q_scale)},
        "k": {"W": np.asarray(L0_attn_k_weight), "scale": np.asarray(L0_attn_k_scale)},
        "v": {"W": np.asarray(L0_attn_v_weight), "scale": np.asarray(L0_attn_v_scale)},
    }, bitnet=True, rope=False)
    ao = run_attn(_heads_q(pr["q"]), _heads_kv(pr["k"]), _heads_vT(pr["v"]), causal=False)
    x = run_out(_gather_heads(ao), np.asarray(L0_attn_o_weight),
                np.asarray(L0_attn_o_scale), x, bitnet=True)

    x = run_ffn_l0(x, np.asarray(L0_ffn_W), np.asarray(L0_ffn_bias),
                   np.asarray(L0_ffn_A), np.asarray(L0_ffn_Bf),
                   np.asarray(L0_ffn_coeffs), np.asarray(L0_ffn_scales))

    cos_full, sin_full = _rope_tables()
    pr = run_qkv(x, {
        "q": {"W": np.asarray(L1_attn_Wq)},
        "k": {"W": np.asarray(L1_attn_Wk)},
        "v": {"W": np.asarray(L1_attn_Wv)},
    }, bitnet=False, rope=True, cos_half=cos_full, sin_half=sin_full)
    ao = run_attn(_heads_q(pr["q"]), _heads_kv(pr["k"]), _heads_vT(pr["v"]), causal=True)
    x = run_out(_gather_heads(ao), np.asarray(L1_attn_Wo), None, x, bitnet=False)

    x = run_ffn_l1(x, np.asarray(L1_ffn_base_w), np.asarray(L1_ffn_base_b),
                   np.asarray(L1_ffn_lora_A), np.asarray(L1_ffn_lora_B),
                   np.asarray(L1_ffn_gate_w))

    return x.reshape(B, S, D)


# version 07
# speedup vs baseline: 1.4095x; 1.4095x over previous
import numpy as np
import ml_dtypes
from contextlib import ExitStack

import concourse.bass as bass
import concourse.mybir as mybir
from concourse.bass_utils import run_bass_kernel_spmd

F32 = mybir.dt.float32
BF16 = mybir.dt.bfloat16
AF = mybir.ActivationFunctionType
ALU = mybir.AluOpType
AX = mybir.AxisListType
BF = ml_dtypes.bfloat16

NC = 8
B, S, D = 2, 4096, 4096
T = B * S
TC = T // NC
KV = 1024
HEADS, KVH, HD = 32, 8, 128
NREP = HEADS // KVH
NQC = S // 128
EPS = 1e-5
ISR = 1.0 / np.sqrt(128.0)
NEG = -3.0e38
NUM_REMAIN = 410

TBS = 512
NTBS = TC // TBS
KWST = [1024, 2048, 3072, 4096]
NBIS = 12
ARENA = 160256



class Chain:

    def __init__(self):
        self.steps = []
        self.tcnt = 0
        self.scnt = 0
        self.gcnt = 0

    def mark(self):
        return (self.tcnt, 0, 0)

    def dma(self, fn, q="sync", wait=None):
        if q == "gpsimd":
            self.steps.append(("gpsimd", "g", 16, fn, wait))
            self.gcnt += 16
        else:
            self.steps.append(("sync", "s", 16, fn, wait))
            self.scnt += 16

    def v(self, fn):
        self.steps.append(("vector", "c", 1, fn, None))
        self.tcnt += 1

    def sc(self, fn):
        self.steps.append(("scalar", "c", 1, fn, None))
        self.tcnt += 1

    def te(self, fn, inc=1):
        self.steps.append(("tensor", "c", inc, fn, None))
        self.tcnt += inc

    def g(self, fn):
        self.steps.append(("gpsimd", "c", 1, fn, None))
        self.tcnt += 1

    def emit(self, nc, block, semc, sems, semg):
        pres = []
        tc = ts = tg = 0
        for (_, lane, inc, _, _) in self.steps:
            pres.append((tc, ts, tg))
            if lane == "c":
                tc += inc
            elif lane == "s":
                ts += inc
            else:
                tg += inc
        totc, tots, totg = tc, ts, tg
        steps = self.steps

        def emit_for(eng_name, e):
            for (eng, lane, inc, fn, wait), (pc, ps, pg) in zip(steps, pres):
                if eng != eng_name:
                    continue
                wc, ws, wg = (pc, ps, pg) if wait is None else wait
                if wc > 0:
                    e.wait_ge(semc, wc)
                if ws > 0 and eng_name != "sync":
                    e.wait_ge(sems, ws)
                if wg > 0 and eng_name != "gpsimd":
                    e.wait_ge(semg, wg)
                inst = fn(e)
                if inc:
                    sem_t = semc if lane == "c" else (sems if lane == "s" else semg)
                    inst.then_inc(sem_t, inc)
            if eng_name == "sync":
                e.wait_ge(semc, totc)
                e.wait_ge(sems, tots)
                if totg:
                    e.wait_ge(semg, totg)

        @block.sync
        def _(e):
            emit_for("sync", e)

        @block.vector
        def _(e):
            emit_for("vector", e)

        @block.scalar
        def _(e):
            emit_for("scalar", e)

        @block.tensor
        def _(e):
            emit_for("tensor", e)

        @block.gpsimd
        def _(e):
            emit_for("gpsimd", e)


def _f32(x):
    return np.ascontiguousarray(x, dtype=np.float32)


IDENT_BF = np.eye(128, dtype=BF)
IDENT_F32 = np.eye(128, dtype=np.float32)


def _chunks_of(c):
    return [c, 15 - c, 16 + c, 31 - c]


def _owner(g):
    if g < 8:
        return g, 0
    if g < 16:
        return 15 - g, 1
    if g < 24:
        return g - 16, 2
    return 31 - g, 3



def build_fused():
    nc = bass.Bass("TRN2", target_bir_lowering=False)
    xT_in = nc.dram_tensor("xT", [D, TC], F32, kind="ExternalInput").ap()
    xo_out = nc.dram_tensor("xoT", [D, TC], F32, kind="ExternalOutput").ap()
    vecs_in = nc.dram_tensor("vecs", [128, 160], F32, kind="ExternalInput").ap()
    identb_in = nc.dram_tensor("identbf", [128, 128], BF16, kind="ExternalInput").ap()
    identf_in = nc.dram_tensor("identf", [128, 128], F32, kind="ExternalInput").ap()
    cos_in = nc.dram_tensor("cos_t", [128, TC], F32, kind="ExternalInput").ap()
    sin_in = nc.dram_tensor("sin_t", [128, TC], F32, kind="ExternalInput").ap()
    mqk_in = nc.dram_tensor("mqk", [128, 4096], BF16, kind="ExternalInput").ap()
    mkq_in = nc.dram_tensor("mkq", [128, 4 * 8 * 128], BF16, kind="ExternalInput").ap()
    win = {}
    for nm, shape in [
        ("WT_q0", [D, 4096]), ("WT_k0", [KV, 4096]), ("WT_v0", [KV, 4096]),
        ("WT_o0", [D, 4096]), ("WT_W0", [D, 4096]),
        ("WT_q1", [D, 4096]), ("WT_k1", [KV, 4096]), ("WT_v1", [KV, 4096]),
        ("WT_o1", [D, 4096]), ("WT_W1", [D, 4096]),
        ("AT0", [256, 4096]), ("AT1", [128, 4096]),
        ("B20", [4096, 256]), ("B21", [4096, 128]),
        ("Wraw_q0", [D, D]), ("Wraw_k0", [KV, D]), ("Wraw_v0", [KV, D]),
        ("Wraw_o0", [D, D]),
        ("coeffs", [16, 1]), ("scales16", [16, 1]), ("M1", [16, 256]),
        ("gwT", [D, 8]), ("M2", [8, 128]),
    ]:
        win[nm] = nc.dram_tensor(nm, shape, F32, kind="ExternalInput").ap()
    q_scr = nc.dram_tensor("q_scr", [D, TC], BF16).ap()
    ao_scr = nc.dram_tensor("ao_scr", [128, 32 * TC], BF16).ap()
    kin = nc.dram_tensor("kin", [KV, TC], BF16)
    vtin = nc.dram_tensor("vtin", [TC, KV], BF16)
    kga = nc.dram_tensor("kga", [NC * KV, TC], BF16)
    vga = nc.dram_tensor("vga", [NC * TC, KV], BF16)

    ch = Chain()
    with ExitStack() as ctx:
        sb = lambda n_, s, d: ctx.enter_context(nc.sbuf_tensor(n_, s, d))

        def at(name, shape, dtype, off):
            return nc.alloc_sbuf_tensor_at(name, shape, dtype, offset=ARENA + off)

        x_res = sb("x_res", [128, 32 * TC], F32)
        wtile = sb("wtile", [128, 4096], BF16)
        obf = sb("obf", [128, 512], BF16)
        svec = sb("svec", [128, 96], F32)
        svtmp = sb("svtmp", [128, 1], F32)
        vecs = sb("vecs_s", [128, 160], F32)
        identb = sb("identb_s", [128, 128], BF16)
        identf = sb("identf_s", [128, 128], F32)
        onesb = sb("onesb", [128, 1], BF16)
        onesf = sb("onesf", [1, 128], F32)
        epsc = sb("epsc", [128, 1], F32)
        eps1 = sb("eps1", [1, 1], F32)
        m_ = sb("m_", [128, 1], F32)
        lo = sb("lo", [128, 1], F32)
        mid = sb("mid", [128, 1], F32)
        hw = sb("hw", [128, 1], F32)
        cnt = sb("cnt", [128, 1], F32)
        rs_ = sb("rs_", [128, 1], F32)
        gw_s = sb("gw_s", [128, 32 * 8], BF16)
        c16 = sb("c16", [16, 1], F32)
        s16 = sb("s16", [16, 1], F32)
        wex = sb("wex", [128, 2], F32)

        RA, RB, RC = 0, 32768, 65536
        nbuf = at("nbuf", [128, 32 * TBS], BF16, RA)
        aTs = at("aTs", [128, 32 * TBS], BF16, RA)
        d_s = at("d_s", [128, 4096], F32, RA)
        wf32A = at("wf32A", [128, 4096], F32, RA)
        m1_s = at("m1_s", [16, 256], F32, RA)
        rows_t = at("rows_t", [1, 4 * TBS], F32, RA)
        rstd_t = at("rstd_t", [128, TBS], F32, RA + 8192)
        nmu_t = at("nmu_t", [128, TBS], F32, RA + 10240)
        tmp_t = at("tmp_t", [128, 512], F32, RA + 12288)
        mqk = at("mqk_s", [128, 4096], BF16, RA + 16384)
        mkq = at("mkq_s", [128, 4 * 8 * 128], BF16, RA + 24576)
        fbuf = at("fbuf", [128, 32 * TBS], BF16, RB)
        wf32B = at("wf32B", [128, 4096], F32, RB)
        kt = at("kt", [128, 4096], BF16, RB)
        vtt = at("vtt", [128, 32 * 129], BF16, RB + 8192)
        taub = at("taub", [128, 512], F32, RB + 16448)
        pind = at("pind", [128, 512], BF16, RB + 18496)
        qh = at("qh", [128, 512], BF16, RB + 19520)
        pbuf = at("pbuf", [128, 512], BF16, RB + 20544)
        cost = at("cost", [128, TC], F32, RB)
        sint = at("sint", [128, TC], F32, RB + 4096)
        rtmp = at("rtmp", [128, TBS], F32, RB + 8192)
        rtmp2 = at("rtmp2", [128, TBS], F32, RB + 10240)
        rsw = at("rsw", [128, TBS], F32, RB + 12288)
        rstd_l = at("rstd_l", [128, TBS], F32, RB + 8192)
        nmu_l = at("nmu_l", [128, TBS], F32, RB + 10240)
        tmp_l = at("tmp_l", [128, 512], F32, RB + 14336)
        rows_l = at("rows_l", [1, 4 * TBS], F32, RB + 16384)
        we_s = at("we_s", [128, 512], F32, RB)
        gl = at("gl", [128, 8], F32, RB + 2048)
        top8 = at("top8", [128, 8], F32, RB + 2112)
        g1 = at("g1", [128, 1], F32, RB + 2176)
        g2 = at("g2", [128, 1], F32, RB + 2240)
        wfull = at("wfull", [128, 8], F32, RB + 2304)
        wf2 = at("wf2", [128, 8], F32, RB + 2368)
        wT_s = at("wT_s", [8, 128], F32, RB + 2432)
        m2_s = at("m2_s", [8, 128], F32, RB + 2944)
        xa_s = at("xa_s", [128, 2 * TBS], BF16, RC)
        taurow = at("taurow", [1, 512], F32, RC)
        strm2 = at("strm2", [128, 512], BF16, RC + 2048)
        cdump = wtile
        assert nc.sbuf_base <= ARENA, ("static pool overlaps arena", nc.sbuf_base)
        assert ARENA + 68608 <= nc.sbuf_top, ("arena overflows SBUF", nc.sbuf_top)

        pqkB = ctx.enter_context(nc.psum_tensor("pqkB", [128, 512], F32))
        povt = [ctx.enter_context(nc.psum_tensor(f"povt{i}", [128, 512], F32))
                for i in range(4)]
        ptrb = ctx.enter_context(nc.psum_tensor("ptrb", [128, 128], BF16))
        pmm = ctx.enter_context(nc.psum_tensor("pmm", [128, 512], F32))
        pbc = ctx.enter_context(nc.psum_tensor("pbc", [128, 512], F32))
        pqkA = pmm
        ptrf = pbc
        semc = ctx.enter_context(nc.semaphore("lane_c"))
        sems = ctx.enter_context(nc.semaphore("lane_s"))
        semg = ctx.enter_context(nc.semaphore("lane_g"))
        block = ctx.enter_context(nc.Block())

        pov = [povt[i][:, 0:129] for i in range(4)]

        wm = [None, None]
        bs_m = [None, None]
        b2_m = [None, None]

        ch.dma(lambda e: e.dma_start(out=identb[:, :], in_=identb_in))
        ch.dma(lambda e: e.dma_start(out=identf[:, :], in_=identf_in))
        ch.dma(lambda e: e.dma_start(out=vecs[:, :], in_=vecs_in))
        ch.dma(lambda e: e.dma_start(out=c16[:, :], in_=win["coeffs"]))
        ch.dma(lambda e: e.dma_start(out=s16[:, :], in_=win["scales16"]))
        ch.dma(lambda e: e.dma_start(out=m1_s[:, :], in_=win["M1"]))
        ch.dma(lambda e: e.dma_start(
            out=gw_s[:, :], in_=win["gwT"].rearrange("(c p) j -> p c j", p=128)),
            q="gpsimd")
        ch.v(lambda e: e.memset(onesb[:, :], 1.0))
        ch.v(lambda e: e.memset(onesf[:, :], 1.0))
        ch.v(lambda e: e.memset(epsc[:, :], EPS))
        ch.v(lambda e: e.memset(eps1[:, :], EPS))
        for i in range(4):
            ch.dma(lambda e, i=i: e.dma_start(
                out=x_res[:, i * 8192:(i + 1) * 8192],
                in_=xT_in.rearrange("(c p) t -> p c t", p=128)[:, i * 8:(i + 1) * 8, :]))
        ch.v(lambda e: e.tensor_tensor(out=c16[:, :], in0=c16[:, :], in1=s16[:, :],
                                       op=ALU.mult))
        for erc in range(2):
            ch.te(lambda e, erc=erc: e.matmul(
                pbc[:, 0:1], m1_s[:, erc * 128:(erc + 1) * 128], c16[:, :],
                start=True, stop=True))
            ch.v(lambda e, erc=erc: e.tensor_copy(out=wex[:, erc:erc + 1], in_=pbc[:, 0:1]))

        def emit_ln_core(srcfn, rows, rstd_b, nmu_b, tmp, resid_dst):
            r_sx = rows[0:1, 0:TBS]
            r_sq = rows[0:1, TBS:2 * TBS]
            r_mu = r_sx
            r_var = r_sq
            r_rstd = rows[0:1, 2 * TBS:3 * TBS]
            r_nmu = rows[0:1, 3 * TBS:4 * TBS]
            for c in range(32):
                src = srcfn(c)
                if resid_dst is None:
                    dst = nbuf[:, c * TBS:(c + 1) * TBS]
                    ch.v(lambda e, src=src, dst=dst: e.tensor_copy(out=dst, in_=src))
                    src = dst
                ch.v(lambda e, src=src: e.tensor_tensor(out=obf[:, 0:TBS], in0=src, in1=src,
                                                        op=ALU.mult))
                ch.te(lambda e, src=src, c=c: e.matmul(
                    pbc[0:1, 0:TBS], onesb[:, :], src, start=(c == 0), stop=(c == 31)), inc=0)
                ch.te(lambda e, c=c: e.matmul(
                    pqkB[0:1, 0:TBS], onesb[:, :], obf[:, 0:TBS],
                    start=(c == 0), stop=(c == 31)), inc=1)
            ch.v(lambda e: e.tensor_copy(out=r_sx, in_=pbc[0:1, 0:TBS]))
            ch.v(lambda e: e.tensor_copy(out=r_sq, in_=pqkB[0:1, 0:TBS]))
            ch.v(lambda e: e.tensor_scalar(
                out=r_mu, in0=r_sx, scalar1=1.0 / D, scalar2=None, op0=ALU.mult))
            ch.v(lambda e: e.tensor_scalar(
                out=r_var, in0=r_sq, scalar1=1.0 / D, scalar2=None, op0=ALU.mult))
            ch.v(lambda e: e.scalar_tensor_tensor(
                out=tmp[0:1, 0:TBS], in0=r_mu, scalar=-1.0, in1=r_mu,
                op0=ALU.mult, op1=ALU.mult))
            ch.v(lambda e: e.tensor_add(r_var, r_var, tmp[0:1, 0:TBS]))
            ch.sc(lambda e: e.activation(r_rstd, r_var, AF.Ln, bias=eps1[:, :]))
            ch.sc(lambda e: e.activation(r_rstd, r_rstd, AF.Exp, scale=-0.5))
            ch.v(lambda e: e.scalar_tensor_tensor(
                out=r_nmu, in0=r_mu, scalar=-1.0, in1=r_rstd, op0=ALU.mult, op1=ALU.mult))
            ch.te(lambda e: e.matmul(pqkB[:, 0:TBS], onesf[:, :], r_rstd,
                                     start=True, stop=True))
            ch.v(lambda e: e.tensor_copy(out=rstd_b[:, :], in_=pqkB[:, 0:TBS]))
            ch.te(lambda e: e.matmul(pqkB[:, 0:TBS], onesf[:, :], r_nmu,
                                     start=True, stop=True))
            ch.v(lambda e: e.tensor_copy(out=nmu_b[:, :], in_=pqkB[:, 0:TBS]))
            for c in range(32):
                if resid_dst is None:
                    dst = nbuf[:, c * TBS:(c + 1) * TBS]
                    ch.v(lambda e, dst=dst: e.tensor_tensor(out=tmp[:, 0:TBS], in0=dst,
                                                            in1=rstd_b[:, :], op=ALU.mult))
                    ch.v(lambda e, dst=dst: e.tensor_add(dst, tmp[:, 0:TBS], nmu_b[:, :]))
                else:
                    src = srcfn(c)
                    dst = resid_dst(c)
                    ch.v(lambda e, src=src: e.tensor_tensor(out=tmp[:, 0:TBS], in0=src,
                                                            in1=rstd_b[:, :], op=ALU.mult))
                    ch.v(lambda e: e.tensor_add(tmp[:, 0:TBS], tmp[:, 0:TBS], nmu_b[:, :]))
                    ch.v(lambda e, dst=dst: e.tensor_add(dst, dst, tmp[:, 0:TBS]))

        def emit_ln(tb):
            emit_ln_core(lambda c: x_res[:, c * TC + tb * TBS: c * TC + tb * TBS + TBS],
                         rows_l, rstd_l, nmu_l, tmp_l, None)

        def emit_tail(tb):
            emit_ln_core(lambda c: fbuf[:, c * TBS:(c + 1) * TBS], rows_t,
                         rstd_t, nmu_t, tmp_t,
                         lambda c: x_res[:, c * TC + tb * TBS: c * TC + tb * TBS + TBS])

        def emit_bstats(wname, nchunks, col0, scale_col0):
            for jc in range(nchunks):
                s_ = jc % 2
                buf = wf32A if s_ == 0 else wf32B
                ch.dma(lambda e, w=win[wname], jc=jc, buf=buf: e.dma_start(
                    out=buf[:, :], in_=w[jc * 128:(jc + 1) * 128, :]),
                    q="sync", wait=bs_m[s_])
                cc = col0 + jc
                ch.v(lambda e, cc=cc, buf=buf: e.tensor_reduce(
                    out=svec[:, cc:cc + 1], in_=buf[:, :], axis=AX.X,
                    op=ALU.add, apply_absolute_value=True))
                ch.v(lambda e, cc=cc: e.tensor_scalar(
                    out=svtmp[:, :], in0=svec[:, cc:cc + 1], scalar1=1.0 / 4096.0,
                    scalar2=1e-8, op0=ALU.mult, op1=ALU.max))
                ch.v(lambda e, cc=cc, sc=scale_col0 + jc: e.tensor_tensor(
                    out=svec[:, cc:cc + 1], in0=svtmp[:, :], in1=vecs[:, sc:sc + 1],
                    op=ALU.mult))
                bs_m[s_] = ch.mark()

        def emit_proj_chunk(wt_ap, jc, bitnet, rhsbuf=None, close=True):
            rb = nbuf if rhsbuf is None else rhsbuf
            for h in range(2):
                ch.dma(lambda e, wt_ap=wt_ap, jc=jc, h=h: e.dma_start(
                    out=wtile[:, h * 2048:(h + 1) * 2048],
                    in_=wt_ap[jc * 128:(jc + 1) * 128, h * 2048:(h + 1) * 2048]),
                    q="gpsimd", wait=wm[h])
                if bitnet:
                    ch.sc(lambda e, h=h: e.activation(
                        wtile[:, h * 2048:(h + 1) * 2048],
                        wtile[:, h * 2048:(h + 1) * 2048], AF.Sign))
                for dc in range(16 * h, 16 * h + 16):
                    last = (dc == 31 and close)
                    ch.te(lambda e, dc=dc, rb=rb, last=last: e.matmul(
                        pmm[:, 0:TBS], wtile[:, dc * 128:(dc + 1) * 128],
                        rb[:, dc * TBS:(dc + 1) * TBS],
                        start=(dc == 0), stop=last), inc=(1 if dc in (15, 31) else 0))
                wm[h] = ch.mark()

        for L in range(2):
            bitnet = (L == 0)
            rope = (L == 1)
            if bitnet:
                emit_bstats("Wraw_q0", 32, 0, 0)
                emit_bstats("Wraw_k0", 8, 32, 32)
                emit_bstats("Wraw_v0", 8, 40, 40)
                emit_bstats("Wraw_o0", 32, 48, 48)
            if rope:
                ch.dma(lambda e: e.dma_start(out=cost[:, :], in_=cos_in))
                ch.dma(lambda e: e.dma_start(out=sint[:, :], in_=sin_in))
            qw = win["WT_q0"] if L == 0 else win["WT_q1"]
            kw = win["WT_k0"] if L == 0 else win["WT_k1"]
            vw = win["WT_v0"] if L == 0 else win["WT_v1"]
            ow = win["WT_o0"] if L == 0 else win["WT_o1"]
            for tb in range(NTBS):
                emit_ln(tb)
                for pname, wt_ap, nj, svcol in (
                        ("q", qw, 32, 0), ("k", kw, 8, 32), ("v", vw, 8, 40)):
                    for jc in range(nj):
                        emit_proj_chunk(wt_ap, jc, bitnet)
                        if bitnet:
                            ch.v(lambda e, cc=svcol + jc: e.tensor_scalar(
                                out=tmp_l[:, 0:TBS], in0=pmm[:, 0:TBS],
                                scalar1=svec[:, cc:cc + 1], scalar2=None, op0=ALU.mult))
                        else:
                            ch.v(lambda e: e.tensor_copy(out=tmp_l[:, 0:TBS],
                                                         in_=pmm[:, 0:TBS]))
                        if rope and pname in ("q", "k"):
                            cs = cost[:, tb * TBS:(tb + 1) * TBS]
                            sn = sint[:, tb * TBS:(tb + 1) * TBS]
                            ch.v(lambda e: e.tensor_scalar(
                                out=rsw[0:64, :], in0=tmp_l[64:128, 0:TBS], scalar1=-1.0,
                                scalar2=None, op0=ALU.mult))
                            ch.v(lambda e: e.tensor_copy(out=rsw[64:128, :],
                                                         in_=tmp_l[0:64, 0:TBS]))
                            ch.v(lambda e, cs=cs: e.tensor_tensor(
                                out=rtmp[:, :], in0=tmp_l[:, 0:TBS], in1=cs, op=ALU.mult))
                            ch.v(lambda e, sn=sn: e.tensor_tensor(
                                out=rtmp2[:, :], in0=rsw[:, :], in1=sn, op=ALU.mult))
                            ch.v(lambda e: e.tensor_add(rtmp[:, :], rtmp[:, :], rtmp2[:, :]))
                            ch.v(lambda e: e.tensor_copy(out=obf[:, 0:TBS], in_=rtmp[:, :]))
                        else:
                            ch.v(lambda e: e.tensor_copy(out=obf[:, 0:TBS],
                                                         in_=tmp_l[:, 0:TBS]))
                        if pname == "q":
                            ch.dma(lambda e, jc=jc, tb=tb: e.dma_start(
                                out=q_scr[jc * 128:(jc + 1) * 128, tb * TBS:(tb + 1) * TBS],
                                in_=obf[:, 0:TBS]))
                        elif pname == "k":
                            ch.dma(lambda e, jc=jc, tb=tb: e.dma_start(
                                out=kin.ap()[jc * 128:(jc + 1) * 128, tb * TBS:(tb + 1) * TBS],
                                in_=obf[:, 0:TBS]))
                        else:
                            for tt in range(TBS // 128):
                                ch.te(lambda e, tt=tt: e.transpose(
                                    ptrb[:, :], obf[:, tt * 128:(tt + 1) * 128], identb[:, :]))
                                ch.v(lambda e: e.tensor_copy(out=pbuf[:, 0:128], in_=ptrb[:, :]))
                                ch.dma(lambda e, jc=jc, tb=tb, tt=tt: e.dma_start(
                                    out=vtin.ap()[tb * TBS + tt * 128: tb * TBS + tt * 128 + 128,
                                                  jc * 128:(jc + 1) * 128],
                                    in_=pbuf[:, 0:128]))
            ch.g(lambda e: e.collective_compute(
                "AllGather", ALU.bypass, replica_groups=[list(range(NC))],
                ins=[kin.ap().opt()], outs=[kga.ap().opt()]))
            ch.g(lambda e: e.collective_compute(
                "AllGather", ALU.bypass, replica_groups=[list(range(NC))],
                ins=[vtin.ap().opt()], outs=[vga.ap().opt()]))
            if L == 1:
                ch.dma(lambda e: e.dma_start(out=mqk[:, :], in_=mqk_in))
                ch.dma(lambda e: e.dma_start(out=mkq[:, :], in_=mkq_in))
            for bb in range(B):
                for kvh in range(KVH):
                    for g in range(32):
                        r_, lci_ = _owner(g)
                        col = (bb * 4 + lci_) * 128
                        ch.dma(lambda e, g=g, r_=r_, col=col, kvh=kvh: e.dma_start(
                            out=kt[:, g * 128:(g + 1) * 128],
                            in_=kga.ap()[r_ * KV + kvh * 128: r_ * KV + (kvh + 1) * 128,
                                         col:col + 128]))
                    ch.v(lambda e: e.memset(vtt[:, :], 1.0))
                    for g in range(32):
                        r_, lci_ = _owner(g)
                        row = r_ * TC + (bb * 4 + lci_) * 128
                        ch.dma(lambda e, g=g, row=row, kvh=kvh: e.dma_start(
                            out=vtt[:, g * 129: g * 129 + 128],
                            in_=vga.ap()[row:row + 128, kvh * 128:(kvh + 1) * 128]))
                    for h4 in range(NREP):
                        h = kvh * NREP + h4
                        ch.dma(lambda e, h=h, bb=bb: e.dma_start(
                            out=qh[:, :], in_=q_scr[h * 128:(h + 1) * 128,
                                                    bb * 512:(bb + 1) * 512]))
                        if L == 1:
                            for lci in range(4):
                                kw_ = KWST[lci]
                                for kb in range(kw_ // 512):
                                    ch.te(lambda e, lci=lci, kb=kb: e.matmul(
                                        pqkA[:, :], qh[:, lci * 128:(lci + 1) * 128],
                                        kt[:, kb * 512:(kb + 1) * 512], start=True, stop=True))
                                    ch.v(lambda e, kb=kb: e.tensor_copy(
                                        out=d_s[:, kb * 512:(kb + 1) * 512], in_=pqkA[:, :]))
                                ch.v(lambda e, kw_=kw_: e.tensor_reduce(
                                    out=lo[:, :], in_=d_s[:, 0:kw_], axis=AX.X, op=ALU.min))
                                bo = 1024 * lci
                                ch.v(lambda e, bo=bo: e.tensor_tensor(
                                    out=d_s[:, bo:bo + 1024], in0=d_s[:, bo:bo + 1024],
                                    in1=mqk[:, bo:bo + 1024], op=ALU.add))
                                ch.v(lambda e, kw_=kw_: e.tensor_reduce(
                                    out=m_[:, :], in_=d_s[:, 0:kw_], axis=AX.X, op=ALU.max))
                                ch.v(lambda e: e.scalar_tensor_tensor(
                                    out=mid[:, :], in0=lo[:, :], scalar=1.0, in1=m_[:, :],
                                    op0=ALU.mult, op1=ALU.add))
                                ch.v(lambda e: e.tensor_scalar(
                                    out=mid[:, :], in0=mid[:, :], scalar1=0.5, scalar2=None,
                                    op0=ALU.mult))
                                ch.v(lambda e: e.tensor_sub(hw[:, :], m_[:, :], mid[:, :]))
                                for it in range(NBIS):
                                    ch.v(lambda e, kw_=kw_: e.tensor_scalar(
                                        out=cdump[:, 0:kw_], in0=d_s[:, 0:kw_],
                                        scalar1=mid[:, :], scalar2=None,
                                        op0=ALU.is_ge, op1=ALU.add, accum_out=cnt[:, :]))
                                    ch.v(lambda e: e.tensor_scalar(
                                        out=cnt[:, :], in0=cnt[:, :], scalar1=float(NUM_REMAIN),
                                        scalar2=2.0, op0=ALU.is_ge, op1=ALU.mult))
                                    ch.v(lambda e: e.scalar_tensor_tensor(
                                        out=cnt[:, :], in0=cnt[:, :], scalar=-1.0, in1=hw[:, :],
                                        op0=ALU.add, op1=ALU.mult))
                                    ch.v(lambda e, sc=float(0.5 ** (it + 1)): e.scalar_tensor_tensor(
                                        out=mid[:, :], in0=cnt[:, :], scalar=sc, in1=mid[:, :],
                                        op0=ALU.mult, op1=ALU.add))
                                ch.v(lambda e, sc=float(4.0 * 0.5 ** NBIS): e.scalar_tensor_tensor(
                                    out=mid[:, :], in0=hw[:, :], scalar=-sc, in1=mid[:, :],
                                    op0=ALU.mult, op1=ALU.add))
                                ch.te(lambda e: e.transpose(ptrf[0:1, 0:128], mid[:, :], identf[:, :]))
                                ch.v(lambda e, lci=lci: e.tensor_copy(
                                    out=taurow[0:1, lci * 128:(lci + 1) * 128],
                                    in_=ptrf[0:1, 0:128]))
                            ch.te(lambda e: e.matmul(pbc[:, 0:512], onesf[:, :],
                                                     taurow[0:1, :], start=True, stop=True))
                            ch.v(lambda e: e.tensor_copy(out=taub[:, :], in_=pbc[:, 0:512]))
                        for kc in range(32):
                            lci_min = 0 if L == 0 else (kc // 8)
                            wq = (4 - lci_min) * 128
                            q0_ = lci_min * 128
                            ch.te(lambda e, kc=kc, q0_=q0_, wq=wq: e.matmul(
                                pqkB[:, 0:wq], kt[:, kc * 128:(kc + 1) * 128],
                                qh[:, q0_:q0_ + wq], start=True, stop=True))
                            if L == 1:
                                mo = ((kc // 8) * 8 + (kc % 8)) * 128
                                ch.v(lambda e, mo=mo: e.tensor_tensor(
                                    out=pqkB[:, 0:128], in0=pqkB[:, 0:128],
                                    in1=mkq[:, mo:mo + 128], op=ALU.add))
                            ch.sc(lambda e, wq=wq: e.activation(
                                pbuf[:, 0:wq], pqkB[:, 0:wq], AF.Exp, scale=ISR))
                            if L == 1:
                                ch.v(lambda e, wq=wq, q0_=q0_: e.scalar_tensor_tensor(
                                    out=pind[:, 0:wq], in0=pqkB[:, 0:wq], scalar=1.0,
                                    in1=taub[:, q0_:q0_ + wq], op0=ALU.mult, op1=ALU.is_ge))
                                ch.v(lambda e, wq=wq: e.tensor_tensor(
                                    out=pbuf[:, 0:wq], in0=pbuf[:, 0:wq],
                                    in1=pind[:, 0:wq], op=ALU.mult))
                            for lci in range(lci_min, 4):
                                kc_last = 31 if L == 0 else (lci * 8 + 7)
                                ch.te(lambda e, lci=lci, lci_min=lci_min, kc=kc,
                                      la=(kc == kc_last): e.matmul(
                                    pov[lci],
                                    pbuf[:, (lci - lci_min) * 128:(lci - lci_min + 1) * 128],
                                    vtt[:, kc * 129:(kc + 1) * 129],
                                    start=(kc == 0), stop=la),
                                    inc=(1 if (lci == 3 or kc == kc_last) else 0))
                        for lci in range(4):
                            g4 = bb * 4 + lci
                            tb_ = g4 // (TBS // 128)
                            tto = (g4 % (TBS // 128)) * 128
                            ch.v(lambda e, lci=lci: e.reciprocal(rs_[:, :], pov[lci][:, 128:129]))
                            ch.v(lambda e, lci=lci: e.tensor_scalar(
                                out=obf[:, 0:128], in0=pov[lci][:, 0:128],
                                scalar1=rs_[:, :], scalar2=None, op0=ALU.mult))
                            ch.te(lambda e: e.transpose(ptrb[:, :], obf[:, 0:128], identb[:, :]))
                            ch.v(lambda e: e.tensor_copy(out=pbuf[:, 0:128], in_=ptrb[:, :]))
                            ch.dma(lambda e, h=h, tb_=tb_, tto=tto: e.dma_start(
                                out=ao_scr[:, (tb_ * 32 + h) * TBS + tto:
                                           (tb_ * 32 + h) * TBS + tto + 128],
                                in_=pbuf[:, 0:128]))
            wm[0] = wm[1] = None
            for tb in range(NTBS):
                ch.dma(lambda e, tb=tb: e.dma_start(
                    out=aTs[:, :], in_=ao_scr[:, tb * 32 * TBS:(tb + 1) * 32 * TBS]))
                for jc in range(32):
                    emit_proj_chunk(ow, jc, bitnet, rhsbuf=aTs)
                    if bitnet:
                        ch.v(lambda e, jc=jc, cc=48 + jc: e.tensor_scalar(
                            out=fbuf[:, jc * TBS:(jc + 1) * TBS], in0=pmm[:, 0:TBS],
                            scalar1=svec[:, cc:cc + 1], scalar2=None, op0=ALU.mult))
                    else:
                        ch.v(lambda e, jc=jc: e.tensor_copy(
                            out=fbuf[:, jc * TBS:(jc + 1) * TBS], in_=pmm[:, 0:TBS]))
                emit_tail(tb)
            wff = win["WT_W0"] if L == 0 else win["WT_W1"]
            at_ap = win["AT0"] if L == 0 else win["AT1"]
            b2_ap = win["B20"] if L == 0 else win["B21"]
            NER = 256 if L == 0 else 128
            NERC = NER // 128
            bias_col0 = 80 if L == 0 else 112
            for tb in range(NTBS):
                emit_ln(tb)
                if L == 1:
                    ch.dma(lambda e: e.dma_start(out=m2_s[:, :], in_=win["M2"]))
                    for tt in range(TBS // 128):
                        for dc in range(32):
                            ch.te(lambda e, dc=dc, tt=tt: e.matmul(
                                pbc[0:128, 0:8],
                                nbuf[:, dc * TBS + tt * 128: dc * TBS + tt * 128 + 128],
                                gw_s[:, dc * 8:(dc + 1) * 8],
                                start=(dc == 0), stop=(dc == 31)), inc=(1 if dc == 31 else 0))
                        ch.v(lambda e: e.tensor_copy(out=gl[:, :], in_=pbc[0:128, 0:8]))
                        ch.v(lambda e: e.max(out=top8[:, :], in_=gl[:, :]))
                        ch.v(lambda e: e.tensor_sub(g1[:, :], top8[:, 1:2], top8[:, 0:1]))
                        ch.sc(lambda e: e.activation(g2[:, :], g1[:, :], AF.Exp))
                        ch.v(lambda e: e.tensor_scalar(out=g1[:, :], in0=g2[:, :], scalar1=1.0,
                                                       scalar2=None, op0=ALU.add))
                        ch.v(lambda e: e.reciprocal(g1[:, :], g1[:, :]))
                        ch.v(lambda e: e.tensor_mul(g2[:, :], g2[:, :], g1[:, :]))
                        ch.v(lambda e: e.tensor_scalar(out=wfull[:, :], in0=gl[:, :],
                                                       scalar1=top8[:, 0:1], scalar2=g1[:, :],
                                                       op0=ALU.is_equal, op1=ALU.mult))
                        ch.v(lambda e: e.tensor_scalar(out=wf2[:, :], in0=gl[:, :],
                                                       scalar1=top8[:, 1:2], scalar2=g2[:, :],
                                                       op0=ALU.is_equal, op1=ALU.mult))
                        ch.v(lambda e: e.tensor_add(wfull[:, :], wfull[:, :], wf2[:, :]))
                        ch.te(lambda e: e.transpose(ptrf[0:8, 0:128], wfull[:, :], identf[:, :]))
                        ch.v(lambda e: e.tensor_copy(out=wT_s[:, :], in_=ptrf[0:8, 0:128]))
                        ch.te(lambda e: e.matmul(pqkB[0:128, 0:128], m2_s[:, :], wT_s[:, :],
                                                 start=True, stop=True))
                        ch.v(lambda e, tt=tt: e.tensor_copy(
                            out=we_s[:, tt * 128:(tt + 1) * 128], in_=pqkB[0:128, 0:128]))
                for erc in range(NERC):
                    for h in range(2):
                        ch.dma(lambda e, at_ap=at_ap, erc=erc, h=h: e.dma_start(
                            out=wtile[:, h * 2048:(h + 1) * 2048],
                            in_=at_ap[erc * 128:(erc + 1) * 128, h * 2048:(h + 1) * 2048]),
                            q="gpsimd", wait=wm[h])
                        for dc in range(16 * h, 16 * h + 16):
                            ch.te(lambda e, dc=dc: e.matmul(
                                pqkA[:, 0:TBS], wtile[:, dc * 128:(dc + 1) * 128],
                                nbuf[:, dc * TBS:(dc + 1) * TBS],
                                start=(dc == 0), stop=(dc == 31)),
                                inc=(1 if dc in (15, 31) else 0))
                        wm[h] = ch.mark()
                    if L == 0:
                        ch.v(lambda e, erc=erc: e.tensor_scalar(
                            out=xa_s[:, erc * TBS:(erc + 1) * TBS], in0=pqkA[:, 0:TBS],
                            scalar1=wex[:, erc:erc + 1], scalar2=None, op0=ALU.mult))
                    else:
                        ch.v(lambda e, erc=erc: e.tensor_tensor(
                            out=xa_s[:, erc * TBS:(erc + 1) * TBS], in0=pqkA[:, 0:TBS],
                            in1=we_s[:, 0:TBS], op=ALU.mult))
                for jc in range(32):
                    slot = jc % 2
                    ch.dma(lambda e, jc=jc, b2_ap=b2_ap, slot=slot, NERC=NERC: e.dma_start(
                        out=strm2[:, slot * 256: slot * 256 + NERC * 128],
                        in_=b2_ap[jc * 128:(jc + 1) * 128, 0:NERC * 128]),
                        q="gpsimd", wait=b2_m[slot])
                    emit_proj_chunk(wff, jc, False, close=False)
                    for erc in range(NERC):
                        ch.te(lambda e, erc=erc, slot=slot: e.matmul(
                            pmm[:, 0:TBS],
                            strm2[:, slot * 256 + erc * 128: slot * 256 + (erc + 1) * 128],
                            xa_s[:, erc * TBS:(erc + 1) * TBS],
                            start=False, stop=(erc == NERC - 1)), inc=1)
                    b2_m[slot] = ch.mark()
                    ch.v(lambda e, jc=jc, cc=bias_col0 + jc: e.tensor_scalar(
                        out=fbuf[:, jc * TBS:(jc + 1) * TBS], in0=pmm[:, 0:TBS],
                        scalar1=vecs[:, cc:cc + 1], scalar2=None, op0=ALU.add))
                emit_tail(tb)
        for i in range(4):
            ch.dma(lambda e, i=i: e.dma_start(
                out=xo_out.rearrange("(c p) t -> p c t", p=128)[:, i * 8:(i + 1) * 8, :],
                in_=x_res[:, i * 8192:(i + 1) * 8192]))
        ch.emit(nc, block, semc, sems, semg)
    return nc


_FUSED_NC = None
_FUSED_CONST = None


def _fused_consts():
    global _FUSED_CONST
    if _FUSED_CONST is not None:
        return _FUSED_CONST
    idx = []
    pos = []
    for c in range(NC):
        ids = []
        ps = []
        for bi in range(B):
            for g in _chunks_of(c):
                ids.extend(range(bi * S + g * 128, bi * S + g * 128 + 128))
                ps.extend(range(g * 128, g * 128 + 128))
        idx.append(np.array(ids, np.int64))
        pos.append(np.array(ps, np.int64))
    i = np.arange(0, 128, 2, dtype=np.float64) / 128.0
    inv32 = (1.0 / (10000.0 ** i)).astype(np.float32)
    t32 = np.arange(S, dtype=np.float32)
    freqs32 = t32[:, None] * inv32[None, :]
    cosf = np.cos(freqs32.astype(np.float64)).astype(np.float32)
    sinf = np.sin(freqs32.astype(np.float64)).astype(np.float32)
    cos_sd = np.concatenate([cosf, cosf], axis=1).T
    sin_sd = np.concatenate([sinf, sinf], axis=1).T
    mqk_l = []
    mkq_l = []
    p_ = np.arange(128)
    for c in range(NC):
        ch_ = _chunks_of(c)
        mq = np.zeros((128, 4096), np.float32)
        for lci in range(4):
            g = ch_[lci]
            k_ = np.arange(1024 * lci, 1024 * (lci + 1))[None, :]
            mq[:, 1024 * lci:1024 * (lci + 1)] = np.where(
                k_ <= g * 128 + p_[:, None], 0.0, NEG)
        mqk_l.append(mq.astype(BF))
        mk = np.zeros((128, 4 * 8 * 128), np.float32)
        for lci in range(4):
            g = ch_[lci]
            for kk in range(8):
                kc = lci * 8 + kk
                q_ = np.arange(lci * 128, lci * 128 + 128)[None, :] - lci * 128
                valid = (kc * 128 + p_[:, None]) <= (g * 128 + q_)
                mk[:, (lci * 8 + kk) * 128:(lci * 8 + kk) * 128 + 128] = \
                    np.where(valid, 0.0, NEG)
        mkq_l.append(mk.astype(BF))
    _FUSED_CONST = (idx, pos, cos_sd, sin_sd, mqk_l, mkq_l)
    return _FUSED_CONST


def _tile_w(W):
    W = np.asarray(W, np.float32)
    od = W.shape[0]
    t = W.reshape(od // 128, 128, 32, 128).transpose(0, 3, 2, 1)
    return _f32(t.reshape(od, 4096))


def _tile_b2(B2):
    B2 = np.asarray(B2, np.float32)
    ner = B2.shape[0]
    t = B2.reshape(ner // 128, 128, 32, 128).transpose(2, 1, 0, 3)
    return _f32(t.reshape(4096, ner))


def kernel(L0_attn_hidden_states, L0_attn_q_weight, L0_attn_q_scale,
           L0_attn_k_weight, L0_attn_k_scale, L0_attn_v_weight, L0_attn_v_scale,
           L0_attn_o_weight, L0_attn_o_scale, L0_ffn_W, L0_ffn_bias, L0_ffn_A,
           L0_ffn_Bf, L0_ffn_coeffs, L0_ffn_scales, L1_attn_Wq, L1_attn_Wk,
           L1_attn_Wv, L1_attn_Wo, L1_ffn_gate_w, L1_ffn_base_w, L1_ffn_base_b,
           L1_ffn_lora_A, L1_ffn_lora_B):
    global _FUSED_NC
    x = _f32(np.asarray(L0_attn_hidden_states).reshape(T, D))
    idx, pos, cos_sd, sin_sd, mqk_l, mkq_l = _fused_consts()
    if _FUSED_NC is None:
        _FUSED_NC = build_fused()
    nc = _FUSED_NC
    tr = lambda a: _f32(np.ascontiguousarray(np.asarray(a, np.float32).T))
    colv = lambda a, n: _f32(np.ascontiguousarray(np.asarray(a, np.float32).reshape(n, 128).T))
    vecs = np.zeros((128, 160), np.float32)
    vecs[:, 0:32] = colv(L0_attn_q_scale, 32)
    vecs[:, 32:40] = colv(L0_attn_k_scale, 8)
    vecs[:, 40:48] = colv(L0_attn_v_scale, 8)
    vecs[:, 48:80] = colv(L0_attn_o_scale, 32)
    vecs[:, 80:112] = colv(L0_ffn_bias, 32)
    vecs[:, 112:144] = colv(L1_ffn_base_b, 32)
    M1 = np.zeros((16, 256), np.float32)
    for e_ in range(16):
        M1[e_, e_ * 16:(e_ + 1) * 16] = 1.0
    M2 = np.zeros((8, 128), np.float32)
    for e_ in range(8):
        M2[e_, e_ * 16:(e_ + 1) * 16] = 2.0
    shared = {
        "vecs": vecs, "identbf": IDENT_BF, "identf": IDENT_F32,
        "WT_q0": _tile_w(L0_attn_q_weight), "WT_k0": _tile_w(L0_attn_k_weight),
        "WT_v0": _tile_w(L0_attn_v_weight), "WT_o0": _tile_w(L0_attn_o_weight),
        "Wraw_q0": _f32(L0_attn_q_weight), "Wraw_k0": _f32(L0_attn_k_weight),
        "Wraw_v0": _f32(L0_attn_v_weight), "Wraw_o0": _f32(L0_attn_o_weight),
        "WT_W0": _tile_w(L0_ffn_W),
        "AT0": _tile_w(np.asarray(L0_ffn_A, np.float32).reshape(256, D)),
        "B20": _tile_b2(np.asarray(L0_ffn_Bf, np.float32).transpose(0, 2, 1).reshape(256, D)),
        "coeffs": _f32(np.asarray(L0_ffn_coeffs).reshape(16, 1)),
        "scales16": _f32(np.asarray(L0_ffn_scales).reshape(16, 1)),
        "M1": M1, "M2": M2,
        "WT_q1": _tile_w(L1_attn_Wq), "WT_k1": _tile_w(L1_attn_Wk),
        "WT_v1": _tile_w(L1_attn_Wv), "WT_o1": _tile_w(L1_attn_Wo),
        "WT_W1": _tile_w(L1_ffn_base_w),
        "AT1": _tile_w(np.asarray(L1_ffn_lora_A, np.float32).reshape(128, D)),
        "B21": _tile_b2(np.asarray(L1_ffn_lora_B, np.float32).transpose(0, 2, 1).reshape(128, D)),
        "gwT": tr(L1_ffn_gate_w),
    }
    in_maps = []
    for c in range(NC):
        m = dict(shared)
        m["xT"] = _f32(np.ascontiguousarray(x[idx[c]].T))
        m["cos_t"] = _f32(np.ascontiguousarray(cos_sd[:, pos[c]]))
        m["sin_t"] = _f32(np.ascontiguousarray(sin_sd[:, pos[c]]))
        m["mqk"] = mqk_l[c]
        m["mkq"] = mkq_l[c]
        in_maps.append(m)
    res = run_bass_kernel_spmd(nc, in_maps, core_ids=list(range(NC))).results
    out = np.empty((T, D), np.float32)
    for c in range(NC):
        out[idx[c]] = np.asarray(res[c]["xoT"], np.float32).T
    return out.reshape(B, S, D)


# version 09 (final)
# speedup vs baseline: 1.8295x; 1.2980x over previous
import numpy as np
import ml_dtypes
from contextlib import ExitStack

import concourse.bass as bass
import concourse.mybir as mybir
from concourse.bass_utils import run_bass_kernel_spmd

F32 = mybir.dt.float32
BF16 = mybir.dt.bfloat16
AF = mybir.ActivationFunctionType
ALU = mybir.AluOpType
AX = mybir.AxisListType
BF = ml_dtypes.bfloat16

NC = 8
B, S, D = 2, 4096, 4096
T = B * S
TC = T // NC
KV = 1024
HEADS, KVH, HD = 32, 8, 128
NREP = HEADS // KVH
NQC = S // 128
EPS = 1e-5
ISR = 1.0 / np.sqrt(128.0)
NEG = -3.0e38
NUM_REMAIN = 410

TBS = 512
NTBS = TC // TBS
KWST = [1024, 2048, 3072, 4096]
NBIS = 12
ARENA = 160256



class Chain:

    def __init__(self):
        self.steps = []
        self.tcnt = 0
        self.scnt = 0
        self.gcnt = 0
        self.acnt = 0

    def mark(self):
        return (self.tcnt, 0, 0, 0)

    def dma(self, fn, q="sync", wait=None):
        if q == "gpsimd":
            self.steps.append(("gpsimd", "g", 16, fn, wait))
            self.gcnt += 16
        else:
            self.steps.append(("sync", "s", 16, fn, wait))
            self.scnt += 16

    def v(self, fn):
        self.steps.append(("vector", "c", 1, fn, None))
        self.tcnt += 1

    def sc(self, fn):
        self.steps.append(("scalar", "c", 1, fn, None))
        self.tcnt += 1

    def a(self, fn, wait=None):
        self.steps.append(("scalar", "a", 1, fn, wait))
        self.acnt += 1

    def te(self, fn, inc=1, wait=None):
        self.steps.append(("tensor", "c", inc, fn, wait))
        self.tcnt += inc

    def g(self, fn):
        self.steps.append(("gpsimd", "c", 1, fn, None))
        self.tcnt += 1

    def emit(self, nc, block, semc, sems, semg, sema):
        pres = []
        tc = ts = tg = ta = 0
        for (_, lane, inc, _, _) in self.steps:
            pres.append((tc, ts, tg, ta))
            if lane == "c":
                tc += inc
            elif lane == "s":
                ts += inc
            elif lane == "g":
                tg += inc
            else:
                ta += inc
        totc, tots, totg, tota = tc, ts, tg, ta
        steps = self.steps
        sem_of = {"c": semc, "s": sems, "g": semg, "a": sema}

        def emit_for(eng_name, e):
            lc = ls = lg = la = 0
            for (eng, lane, inc, fn, wait), (pc, ps, pg, pa) in zip(steps, pres):
                if eng != eng_name:
                    continue
                if wait is None:
                    wc, ws, wg, wa = pc, ps, pg, pa
                else:
                    wc = pc if wait[0] is None else wait[0]
                    ws = ps if wait[1] is None else wait[1]
                    wg = pg if wait[2] is None else wait[2]
                    wa = pa if wait[3] is None else wait[3]
                if wc > lc:
                    e.wait_ge(semc, wc)
                    lc = wc
                if ws > ls and eng_name != "sync":
                    e.wait_ge(sems, ws)
                    ls = ws
                if wg > lg and eng_name != "gpsimd":
                    e.wait_ge(semg, wg)
                    lg = wg
                if wa > la and eng_name != "scalar":
                    e.wait_ge(sema, wa)
                    la = wa
                inst = fn(e)
                if inc:
                    inst.then_inc(sem_of[lane], inc)
            if eng_name == "sync":
                e.wait_ge(semc, totc)
                e.wait_ge(sems, tots)
                if totg:
                    e.wait_ge(semg, totg)
                if tota:
                    e.wait_ge(sema, tota)

        @block.sync
        def _(e):
            emit_for("sync", e)

        @block.vector
        def _(e):
            emit_for("vector", e)

        @block.scalar
        def _(e):
            emit_for("scalar", e)

        @block.tensor
        def _(e):
            emit_for("tensor", e)

        @block.gpsimd
        def _(e):
            emit_for("gpsimd", e)


def _f32(x):
    return np.ascontiguousarray(x, dtype=np.float32)


IDENT_BF = np.eye(128, dtype=BF)
IDENT_F32 = np.eye(128, dtype=np.float32)


def _chunks_of(c):
    return [c, 15 - c, 16 + c, 31 - c]


def _owner(g):
    if g < 8:
        return g, 0
    if g < 16:
        return 15 - g, 1
    if g < 24:
        return g - 16, 2
    return 31 - g, 3



def build_fused():
    nc = bass.Bass("TRN2", target_bir_lowering=False)
    xT_in = nc.dram_tensor("xT", [D, TC], F32, kind="ExternalInput").ap()
    xo_out = nc.dram_tensor("xoT", [D, TC], F32, kind="ExternalOutput").ap()
    vecs_in = nc.dram_tensor("vecs", [128, 160], F32, kind="ExternalInput").ap()
    identb_in = nc.dram_tensor("identbf", [128, 128], BF16, kind="ExternalInput").ap()
    identf_in = nc.dram_tensor("identf", [128, 128], F32, kind="ExternalInput").ap()
    cos_in = nc.dram_tensor("cos_t", [128, TC], F32, kind="ExternalInput").ap()
    sin_in = nc.dram_tensor("sin_t", [128, TC], F32, kind="ExternalInput").ap()
    mqk_in = nc.dram_tensor("mqk", [128, 4096], BF16, kind="ExternalInput").ap()
    mkq_in = nc.dram_tensor("mkq", [128, 4 * 8 * 128], BF16, kind="ExternalInput").ap()
    win = {}
    for nm, shape in [
        ("WT_q0", [D, 4096]), ("WT_k0", [KV, 4096]), ("WT_v0", [KV, 4096]),
        ("WT_o0", [D, 4096]), ("WT_W0", [D, 4096]),
        ("WT_q1", [D, 4096]), ("WT_k1", [KV, 4096]), ("WT_v1", [KV, 4096]),
        ("WT_o1", [D, 4096]), ("WT_W1", [D, 4096]),
        ("AT0", [256, 4096]), ("AT1", [128, 4096]),
        ("B20", [4096, 256]), ("B21", [4096, 128]),
        ("Wraw_q0", [D, D]), ("Wraw_k0", [KV, D]), ("Wraw_v0", [KV, D]),
        ("Wraw_o0", [D, D]),
        ("coeffs", [16, 1]), ("scales16", [16, 1]), ("M1", [16, 256]),
        ("gwT", [D, 8]), ("M2", [8, 128]),
    ]:
        win[nm] = nc.dram_tensor(nm, shape, F32, kind="ExternalInput").ap()
    q_scr = nc.dram_tensor("q_scr", [D, TC], BF16).ap()
    ao_scr = nc.dram_tensor("ao_scr", [128, 32 * TC], BF16).ap()
    kin = nc.dram_tensor("kin", [KV, TC], BF16)
    vtin = nc.dram_tensor("vtin", [TC, KV], BF16)
    kga = nc.dram_tensor("kga", [NC * KV, TC], BF16)
    vga = nc.dram_tensor("vga", [NC * TC, KV], BF16)

    ch = Chain()
    with ExitStack() as ctx:
        sb = lambda n_, s, d: ctx.enter_context(nc.sbuf_tensor(n_, s, d))

        def at(name, shape, dtype, off):
            return nc.alloc_sbuf_tensor_at(name, shape, dtype, offset=ARENA + off)

        x_res = sb("x_res", [128, 32 * TC], F32)
        wtile = sb("wtile", [128, 4096], BF16)
        obf = sb("obf", [128, 512], BF16)
        svec = sb("svec", [128, 96], F32)
        svtmp = sb("svtmp", [128, 1], F32)
        vecs = sb("vecs_s", [128, 160], F32)
        identb = sb("identb_s", [128, 128], BF16)
        identf = sb("identf_s", [128, 128], F32)
        onesb = sb("onesb", [128, 1], BF16)
        onesf = sb("onesf", [1, 128], F32)
        epsc = sb("epsc", [128, 1], F32)
        eps1 = sb("eps1", [1, 1], F32)
        m_ = sb("m_", [128, 1], F32)
        lo = sb("lo", [128, 1], F32)
        mid = sb("mid", [128, 1], F32)
        hw = sb("hw", [128, 1], F32)
        cnt = sb("cnt", [128, 1], F32)
        rs_ = sb("rs_", [128, 1], F32)
        gw_s = sb("gw_s", [128, 32 * 8], BF16)
        c16 = sb("c16", [16, 1], F32)
        s16 = sb("s16", [16, 1], F32)
        wex = sb("wex", [128, 2], F32)

        RA, RB, RC = 0, 32768, 65536
        nbuf = at("nbuf", [128, 32 * TBS], BF16, RA)
        aTs = at("aTs", [128, 32 * TBS], BF16, RA)
        d_s = at("d_s", [128, 4096], F32, RA)
        wf32A = at("wf32A", [128, 4096], F32, RA)
        m1_s = at("m1_s", [16, 256], F32, RA)
        rows_t = at("rows_t", [1, 4 * TBS], F32, RA)
        rstd_t = at("rstd_t", [128, TBS], F32, RA + 8192)
        nmu_t = at("nmu_t", [128, TBS], F32, RA + 10240)
        tmp_t = at("tmp_t", [128, 512], F32, RA + 12288)
        mqk = at("mqk_s", [128, 4096], BF16, RA + 16384)
        mkq = at("mkq_s", [128, 4 * 8 * 128], BF16, RA + 24576)
        fbuf = at("fbuf", [128, 32 * TBS], BF16, RB)
        wf32B = at("wf32B", [128, 4096], F32, RB)
        kt = at("kt", [128, 4096], BF16, RB)
        vtt = at("vtt", [128, 32 * 129], BF16, RB + 8192)
        taub = at("taub", [128, 512], F32, RB + 16448)
        pind = at("pind", [128, 512], BF16, RB + 18496)
        qh = at("qh", [128, 512], BF16, RB + 19520)
        pbuf = at("pbuf", [128, 512], BF16, RB + 20544)
        cost = at("cost", [128, TC], F32, RB)
        sint = at("sint", [128, TC], F32, RB + 4096)
        rtmp = at("rtmp", [128, TBS], F32, RB + 8192)
        rtmp2 = at("rtmp2", [128, TBS], F32, RB + 10240)
        rsw = at("rsw", [128, TBS], F32, RB + 12288)
        rstd_l = at("rstd_l", [128, TBS], F32, RB + 8192)
        nmu_l = at("nmu_l", [128, TBS], F32, RB + 10240)
        tmp_l = at("tmp_l", [128, 512], F32, RB + 14336)
        rows_l = at("rows_l", [1, 4 * TBS], F32, RB + 16384)
        we_s = at("we_s", [128, 512], F32, RB)
        gl = at("gl", [128, 8], F32, RB + 2048)
        top8 = at("top8", [128, 8], F32, RB + 2112)
        g1 = at("g1", [128, 1], F32, RB + 2176)
        g2 = at("g2", [128, 1], F32, RB + 2240)
        wfull = at("wfull", [128, 8], F32, RB + 2304)
        wf2 = at("wf2", [128, 8], F32, RB + 2368)
        wT_s = at("wT_s", [8, 128], F32, RB + 2432)
        m2_s = at("m2_s", [8, 128], F32, RB + 2944)
        xa_s = at("xa_s", [128, 2 * TBS], BF16, RC)
        taurow = at("taurow", [1, 512], F32, RC)
        strm2 = at("strm2", [128, 512], BF16, RC + 2048)
        cdump = wtile
        assert nc.sbuf_base <= ARENA, ("static pool overlaps arena", nc.sbuf_base)
        assert ARENA + 68608 <= nc.sbuf_top, ("arena overflows SBUF", nc.sbuf_top)

        pqkB = ctx.enter_context(nc.psum_tensor("pqkB", [128, 512], F32))
        povt = [ctx.enter_context(nc.psum_tensor(f"povt{i}", [128, 512], F32))
                for i in range(4)]
        ptrb = ctx.enter_context(nc.psum_tensor("ptrb", [128, 128], BF16))
        pmm = ctx.enter_context(nc.psum_tensor("pmm", [128, 512], F32))
        pbc = ctx.enter_context(nc.psum_tensor("pbc", [128, 512], F32))
        pqkA = pmm
        ptrf = pbc
        semc = ctx.enter_context(nc.semaphore("lane_c"))
        sems = ctx.enter_context(nc.semaphore("lane_s"))
        semg = ctx.enter_context(nc.semaphore("lane_g"))
        sema = ctx.enter_context(nc.semaphore("lane_a"))
        block = ctx.enter_context(nc.Block())

        pov = [povt[i][:, 0:129] for i in range(4)]

        wm = [None, None]
        bs_m = [None, None]
        b2_m = [None, None]
        amark = [None, None]

        ch.dma(lambda e: e.dma_start(out=identb[:, :], in_=identb_in))
        ch.dma(lambda e: e.dma_start(out=identf[:, :], in_=identf_in))
        ch.dma(lambda e: e.dma_start(out=vecs[:, :], in_=vecs_in))
        ch.dma(lambda e: e.dma_start(out=c16[:, :], in_=win["coeffs"]))
        ch.dma(lambda e: e.dma_start(out=s16[:, :], in_=win["scales16"]))
        ch.dma(lambda e: e.dma_start(out=m1_s[:, :], in_=win["M1"]))
        ch.dma(lambda e: e.dma_start(
            out=gw_s[:, :], in_=win["gwT"].rearrange("(c p) j -> p c j", p=128)),
            q="gpsimd")
        ch.v(lambda e: e.memset(onesb[:, :], 1.0))
        ch.v(lambda e: e.memset(onesf[:, :], 1.0))
        ch.v(lambda e: e.memset(epsc[:, :], EPS))
        ch.v(lambda e: e.memset(eps1[:, :], EPS))
        for i in range(4):
            ch.dma(lambda e, i=i: e.dma_start(
                out=x_res[:, i * 8192:(i + 1) * 8192],
                in_=xT_in.rearrange("(c p) t -> p c t", p=128)[:, i * 8:(i + 1) * 8, :]))
        ch.v(lambda e: e.tensor_tensor(out=c16[:, :], in0=c16[:, :], in1=s16[:, :],
                                       op=ALU.mult))
        for erc in range(2):
            ch.te(lambda e, erc=erc: e.matmul(
                pbc[:, 0:1], m1_s[:, erc * 128:(erc + 1) * 128], c16[:, :],
                start=True, stop=True))
            ch.v(lambda e, erc=erc: e.tensor_copy(out=wex[:, erc:erc + 1], in_=pbc[:, 0:1]))

        def emit_ln_core(srcfn, rows, rstd_b, nmu_b, tmp, resid_dst):
            r_sx = rows[0:1, 0:TBS]
            r_sq = rows[0:1, TBS:2 * TBS]
            r_mu = r_sx
            r_var = r_sq
            r_rstd = rows[0:1, 2 * TBS:3 * TBS]
            r_nmu = rows[0:1, 3 * TBS:4 * TBS]
            for c in range(32):
                src = srcfn(c)
                if resid_dst is None:
                    dst = nbuf[:, c * TBS:(c + 1) * TBS]
                    ch.v(lambda e, src=src, dst=dst: e.tensor_copy(out=dst, in_=src))
                    src = dst
                ch.v(lambda e, src=src: e.tensor_tensor(out=obf[:, 0:TBS], in0=src, in1=src,
                                                        op=ALU.mult))
                ch.te(lambda e, src=src, c=c: e.matmul(
                    pbc[0:1, 0:TBS], onesb[:, :], src, start=(c == 0), stop=(c == 31)), inc=0)
                ch.te(lambda e, c=c: e.matmul(
                    pqkB[0:1, 0:TBS], onesb[:, :], obf[:, 0:TBS],
                    start=(c == 0), stop=(c == 31)), inc=1)
            ch.v(lambda e: e.tensor_copy(out=r_sx, in_=pbc[0:1, 0:TBS]))
            ch.v(lambda e: e.tensor_copy(out=r_sq, in_=pqkB[0:1, 0:TBS]))
            ch.v(lambda e: e.tensor_scalar(
                out=r_mu, in0=r_sx, scalar1=1.0 / D, scalar2=None, op0=ALU.mult))
            ch.v(lambda e: e.tensor_scalar(
                out=r_var, in0=r_sq, scalar1=1.0 / D, scalar2=None, op0=ALU.mult))
            ch.v(lambda e: e.scalar_tensor_tensor(
                out=tmp[0:1, 0:TBS], in0=r_mu, scalar=-1.0, in1=r_mu,
                op0=ALU.mult, op1=ALU.mult))
            ch.v(lambda e: e.tensor_add(r_var, r_var, tmp[0:1, 0:TBS]))
            ch.sc(lambda e: e.activation(r_rstd, r_var, AF.Ln, bias=eps1[:, :]))
            ch.sc(lambda e: e.activation(r_rstd, r_rstd, AF.Exp, scale=-0.5))
            ch.v(lambda e: e.scalar_tensor_tensor(
                out=r_nmu, in0=r_mu, scalar=-1.0, in1=r_rstd, op0=ALU.mult, op1=ALU.mult))
            ch.te(lambda e: e.matmul(pqkB[:, 0:TBS], onesf[:, :], r_rstd,
                                     start=True, stop=True))
            ch.v(lambda e: e.tensor_copy(out=rstd_b[:, :], in_=pqkB[:, 0:TBS]))
            ch.te(lambda e: e.matmul(pqkB[:, 0:TBS], onesf[:, :], r_nmu,
                                     start=True, stop=True))
            ch.v(lambda e: e.tensor_copy(out=nmu_b[:, :], in_=pqkB[:, 0:TBS]))
            for c in range(32):
                if resid_dst is None:
                    dst = nbuf[:, c * TBS:(c + 1) * TBS]
                    ch.v(lambda e, dst=dst: e.tensor_tensor(out=tmp[:, 0:TBS], in0=dst,
                                                            in1=rstd_b[:, :], op=ALU.mult))
                    ch.v(lambda e, dst=dst: e.tensor_add(dst, tmp[:, 0:TBS], nmu_b[:, :]))
                else:
                    src = srcfn(c)
                    dst = resid_dst(c)
                    ch.v(lambda e, src=src: e.tensor_tensor(out=tmp[:, 0:TBS], in0=src,
                                                            in1=rstd_b[:, :], op=ALU.mult))
                    ch.v(lambda e: e.tensor_add(tmp[:, 0:TBS], tmp[:, 0:TBS], nmu_b[:, :]))
                    ch.v(lambda e, dst=dst: e.tensor_add(dst, dst, tmp[:, 0:TBS]))

        def emit_ln(tb):
            emit_ln_core(lambda c: x_res[:, c * TC + tb * TBS: c * TC + tb * TBS + TBS],
                         rows_l, rstd_l, nmu_l, tmp_l, None)

        def emit_tail(tb):
            emit_ln_core(lambda c: fbuf[:, c * TBS:(c + 1) * TBS], rows_t,
                         rstd_t, nmu_t, tmp_t,
                         lambda c: x_res[:, c * TC + tb * TBS: c * TC + tb * TBS + TBS])

        def emit_bstats(wname, nchunks, col0, scale_col0):
            for jc in range(nchunks):
                s_ = jc % 2
                buf = wf32A if s_ == 0 else wf32B
                ch.dma(lambda e, w=win[wname], jc=jc, buf=buf: e.dma_start(
                    out=buf[:, :], in_=w[jc * 128:(jc + 1) * 128, :]),
                    q="sync", wait=bs_m[s_])
                cc = col0 + jc
                ch.v(lambda e, cc=cc, buf=buf: e.tensor_reduce(
                    out=svec[:, cc:cc + 1], in_=buf[:, :], axis=AX.X,
                    op=ALU.add, apply_absolute_value=True))
                ch.v(lambda e, cc=cc: e.tensor_scalar(
                    out=svtmp[:, :], in0=svec[:, cc:cc + 1], scalar1=1.0 / 4096.0,
                    scalar2=1e-8, op0=ALU.mult, op1=ALU.max))
                ch.v(lambda e, cc=cc, sc=scale_col0 + jc: e.tensor_tensor(
                    out=svec[:, cc:cc + 1], in0=svtmp[:, :], in1=vecs[:, sc:sc + 1],
                    op=ALU.mult))
                bs_m[s_] = ch.mark()

        def emit_proj_chunk(wt_ap, jc, bitnet, rhsbuf=None, close=True):
            rb = nbuf if rhsbuf is None else rhsbuf
            for h in range(2):
                ch.dma(lambda e, wt_ap=wt_ap, jc=jc, h=h: e.dma_start(
                    out=wtile[:, h * 2048:(h + 1) * 2048],
                    in_=wt_ap[jc * 128:(jc + 1) * 128, h * 2048:(h + 1) * 2048]),
                    q="gpsimd", wait=wm[h])
                if bitnet:
                    ch.sc(lambda e, h=h: e.activation(
                        wtile[:, h * 2048:(h + 1) * 2048],
                        wtile[:, h * 2048:(h + 1) * 2048], AF.Sign))
                for dc in range(16 * h, 16 * h + 16):
                    last = (dc == 31 and close)
                    ch.te(lambda e, dc=dc, rb=rb, last=last: e.matmul(
                        pmm[:, 0:TBS], wtile[:, dc * 128:(dc + 1) * 128],
                        rb[:, dc * TBS:(dc + 1) * TBS],
                        start=(dc == 0), stop=last), inc=(1 if dc in (15, 31) else 0))
                wm[h] = ch.mark()

        for L in range(2):
            bitnet = (L == 0)
            rope = (L == 1)
            if bitnet:
                emit_bstats("Wraw_q0", 32, 0, 0)
                emit_bstats("Wraw_k0", 8, 32, 32)
                emit_bstats("Wraw_v0", 8, 40, 40)
                emit_bstats("Wraw_o0", 32, 48, 48)
            if rope:
                ch.dma(lambda e: e.dma_start(out=cost[:, :], in_=cos_in))
                ch.dma(lambda e: e.dma_start(out=sint[:, :], in_=sin_in))
            qw = win["WT_q0"] if L == 0 else win["WT_q1"]
            kw = win["WT_k0"] if L == 0 else win["WT_k1"]
            vw = win["WT_v0"] if L == 0 else win["WT_v1"]
            ow = win["WT_o0"] if L == 0 else win["WT_o1"]
            for tb in range(NTBS):
                emit_ln(tb)
                for pname, wt_ap, nj, svcol in (
                        ("q", qw, 32, 0), ("k", kw, 8, 32), ("v", vw, 8, 40)):
                    for jc in range(nj):
                        emit_proj_chunk(wt_ap, jc, bitnet)
                        if bitnet:
                            ch.v(lambda e, cc=svcol + jc: e.tensor_scalar(
                                out=tmp_l[:, 0:TBS], in0=pmm[:, 0:TBS],
                                scalar1=svec[:, cc:cc + 1], scalar2=None, op0=ALU.mult))
                        else:
                            ch.v(lambda e: e.tensor_copy(out=tmp_l[:, 0:TBS],
                                                         in_=pmm[:, 0:TBS]))
                        if rope and pname in ("q", "k"):
                            cs = cost[:, tb * TBS:(tb + 1) * TBS]
                            sn = sint[:, tb * TBS:(tb + 1) * TBS]
                            ch.v(lambda e: e.tensor_scalar(
                                out=rsw[0:64, :], in0=tmp_l[64:128, 0:TBS], scalar1=-1.0,
                                scalar2=None, op0=ALU.mult))
                            ch.v(lambda e: e.tensor_copy(out=rsw[64:128, :],
                                                         in_=tmp_l[0:64, 0:TBS]))
                            ch.v(lambda e, cs=cs: e.tensor_tensor(
                                out=rtmp[:, :], in0=tmp_l[:, 0:TBS], in1=cs, op=ALU.mult))
                            ch.v(lambda e, sn=sn: e.tensor_tensor(
                                out=rtmp2[:, :], in0=rsw[:, :], in1=sn, op=ALU.mult))
                            ch.v(lambda e: e.tensor_add(rtmp[:, :], rtmp[:, :], rtmp2[:, :]))
                            ch.v(lambda e: e.tensor_copy(out=obf[:, 0:TBS], in_=rtmp[:, :]))
                        else:
                            ch.v(lambda e: e.tensor_copy(out=obf[:, 0:TBS],
                                                         in_=tmp_l[:, 0:TBS]))
                        if pname == "q":
                            ch.dma(lambda e, jc=jc, tb=tb: e.dma_start(
                                out=q_scr[jc * 128:(jc + 1) * 128, tb * TBS:(tb + 1) * TBS],
                                in_=obf[:, 0:TBS]))
                        elif pname == "k":
                            ch.dma(lambda e, jc=jc, tb=tb: e.dma_start(
                                out=kin.ap()[jc * 128:(jc + 1) * 128, tb * TBS:(tb + 1) * TBS],
                                in_=obf[:, 0:TBS]))
                        else:
                            for tt in range(TBS // 128):
                                ch.te(lambda e, tt=tt: e.transpose(
                                    ptrb[:, :], obf[:, tt * 128:(tt + 1) * 128], identb[:, :]))
                                ch.v(lambda e: e.tensor_copy(out=pbuf[:, 0:128], in_=ptrb[:, :]))
                                ch.dma(lambda e, jc=jc, tb=tb, tt=tt: e.dma_start(
                                    out=vtin.ap()[tb * TBS + tt * 128: tb * TBS + tt * 128 + 128,
                                                  jc * 128:(jc + 1) * 128],
                                    in_=pbuf[:, 0:128]))
            ch.g(lambda e: e.collective_compute(
                "AllGather", ALU.bypass, replica_groups=[list(range(NC))],
                ins=[kin.ap().opt()], outs=[kga.ap().opt()]))
            ch.g(lambda e: e.collective_compute(
                "AllGather", ALU.bypass, replica_groups=[list(range(NC))],
                ins=[vtin.ap().opt()], outs=[vga.ap().opt()]))
            if L == 1:
                ch.dma(lambda e: e.dma_start(out=mqk[:, :], in_=mqk_in))
                ch.dma(lambda e: e.dma_start(out=mkq[:, :], in_=mkq_in))
            for bb in range(B):
                for kvh in range(KVH):
                    for g in range(32):
                        r_, lci_ = _owner(g)
                        col = (bb * 4 + lci_) * 128
                        ch.dma(lambda e, g=g, r_=r_, col=col, kvh=kvh: e.dma_start(
                            out=kt[:, g * 128:(g + 1) * 128],
                            in_=kga.ap()[r_ * KV + kvh * 128: r_ * KV + (kvh + 1) * 128,
                                         col:col + 128]))
                    ch.v(lambda e: e.memset(vtt[:, :], 1.0))
                    for g in range(32):
                        r_, lci_ = _owner(g)
                        row = r_ * TC + (bb * 4 + lci_) * 128
                        ch.dma(lambda e, g=g, row=row, kvh=kvh: e.dma_start(
                            out=vtt[:, g * 129: g * 129 + 128],
                            in_=vga.ap()[row:row + 128, kvh * 128:(kvh + 1) * 128]))
                    for h4 in range(NREP):
                        h = kvh * NREP + h4
                        ch.dma(lambda e, h=h, bb=bb: e.dma_start(
                            out=qh[:, :], in_=q_scr[h * 128:(h + 1) * 128,
                                                    bb * 512:(bb + 1) * 512]))
                        if L == 1:
                            for lci in range(4):
                                kw_ = KWST[lci]
                                for kb in range(kw_ // 512):
                                    sb_ = pqkB if kb % 2 == 0 else pmm
                                    ch.te(lambda e, lci=lci, kb=kb, sb_=sb_: e.matmul(
                                        sb_[:, 0:512], qh[:, lci * 128:(lci + 1) * 128],
                                        kt[:, kb * 512:(kb + 1) * 512], start=True, stop=True),
                                        inc=1, wait=(None, None, None, amark[kb % 2]))
                                    mqk_c = ch.tcnt
                                    ch.a(lambda e, kb=kb, sb_=sb_: e.activation(
                                        d_s[:, kb * 512:(kb + 1) * 512], sb_[:, 0:512],
                                        AF.Identity),
                                        wait=(mqk_c, None, None, None))
                                    amark[kb % 2] = ch.acnt
                                ch.v(lambda e, kw_=kw_: e.tensor_reduce(
                                    out=lo[:, :], in_=d_s[:, 0:kw_], axis=AX.X, op=ALU.min))
                                bo = 1024 * lci
                                ch.v(lambda e, bo=bo: e.tensor_tensor(
                                    out=d_s[:, bo:bo + 1024], in0=d_s[:, bo:bo + 1024],
                                    in1=mqk[:, bo:bo + 1024], op=ALU.add))
                                ch.v(lambda e, kw_=kw_: e.tensor_reduce(
                                    out=m_[:, :], in_=d_s[:, 0:kw_], axis=AX.X, op=ALU.max))
                                ch.v(lambda e: e.scalar_tensor_tensor(
                                    out=mid[:, :], in0=lo[:, :], scalar=1.0, in1=m_[:, :],
                                    op0=ALU.mult, op1=ALU.add))
                                ch.v(lambda e: e.tensor_scalar(
                                    out=mid[:, :], in0=mid[:, :], scalar1=0.5, scalar2=None,
                                    op0=ALU.mult))
                                ch.v(lambda e: e.tensor_sub(hw[:, :], m_[:, :], mid[:, :]))
                                for it in range(NBIS):
                                    ch.v(lambda e, kw_=kw_: e.tensor_scalar(
                                        out=cdump[:, 0:kw_], in0=d_s[:, 0:kw_],
                                        scalar1=mid[:, :], scalar2=None,
                                        op0=ALU.is_ge, op1=ALU.add, accum_out=cnt[:, :]))
                                    ch.v(lambda e: e.tensor_scalar(
                                        out=cnt[:, :], in0=cnt[:, :], scalar1=float(NUM_REMAIN),
                                        scalar2=2.0, op0=ALU.is_ge, op1=ALU.mult))
                                    ch.v(lambda e: e.scalar_tensor_tensor(
                                        out=cnt[:, :], in0=cnt[:, :], scalar=-1.0, in1=hw[:, :],
                                        op0=ALU.add, op1=ALU.mult))
                                    ch.v(lambda e, sc=float(0.5 ** (it + 1)): e.scalar_tensor_tensor(
                                        out=mid[:, :], in0=cnt[:, :], scalar=sc, in1=mid[:, :],
                                        op0=ALU.mult, op1=ALU.add))
                                ch.v(lambda e, sc=float(4.0 * 0.5 ** NBIS): e.scalar_tensor_tensor(
                                    out=mid[:, :], in0=hw[:, :], scalar=-sc, in1=mid[:, :],
                                    op0=ALU.mult, op1=ALU.add))
                                ch.te(lambda e: e.transpose(ptrf[0:1, 0:128], mid[:, :], identf[:, :]))
                                ch.v(lambda e, lci=lci: e.tensor_copy(
                                    out=taurow[0:1, lci * 128:(lci + 1) * 128],
                                    in_=ptrf[0:1, 0:128]))
                            ch.te(lambda e: e.matmul(pbc[:, 0:512], onesf[:, :],
                                                     taurow[0:1, :], start=True, stop=True))
                            ch.v(lambda e: e.tensor_copy(out=taub[:, :], in_=pbc[:, 0:512]))
                        for kc in range(32):
                            if L == 0:
                                sb_ = pqkB if kc % 2 == 0 else pmm
                                ps_ = pbuf if kc % 2 == 0 else pind
                                ch.te(lambda e, kc=kc, sb_=sb_: e.matmul(
                                    sb_[:, 0:512], kt[:, kc * 128:(kc + 1) * 128],
                                    qh[:, 0:512], start=True, stop=True),
                                    inc=1, wait=(None, None, None, amark[kc % 2]))
                                mqk_c = ch.tcnt
                                ch.a(lambda e, sb_=sb_, ps_=ps_: e.activation(
                                    ps_[:, 0:512], sb_[:, 0:512], AF.Exp, scale=ISR),
                                    wait=(mqk_c, None, None, None))
                                amark[kc % 2] = ch.acnt
                                for lci in range(4):
                                    ch.te(lambda e, lci=lci, kc=kc, ps_=ps_: e.matmul(
                                        pov[lci],
                                        ps_[:, lci * 128:(lci + 1) * 128],
                                        vtt[:, kc * 129:(kc + 1) * 129],
                                        start=(kc == 0), stop=(kc == 31)),
                                        inc=(1 if lci == 3 else 0))
                                continue
                            lci_min = kc // 8
                            wq = (4 - lci_min) * 128
                            q0_ = lci_min * 128
                            ch.te(lambda e, kc=kc, q0_=q0_, wq=wq: e.matmul(
                                pqkB[:, 0:wq], kt[:, kc * 128:(kc + 1) * 128],
                                qh[:, q0_:q0_ + wq], start=True, stop=True))
                            mo = ((kc // 8) * 8 + (kc % 8)) * 128
                            ch.v(lambda e, mo=mo: e.tensor_tensor(
                                out=pqkB[:, 0:128], in0=pqkB[:, 0:128],
                                in1=mkq[:, mo:mo + 128], op=ALU.add))
                            ch.sc(lambda e, wq=wq: e.activation(
                                pbuf[:, 0:wq], pqkB[:, 0:wq], AF.Exp, scale=ISR))
                            ch.v(lambda e, wq=wq, q0_=q0_: e.scalar_tensor_tensor(
                                out=pind[:, 0:wq], in0=pqkB[:, 0:wq], scalar=1.0,
                                in1=taub[:, q0_:q0_ + wq], op0=ALU.mult, op1=ALU.is_ge))
                            ch.v(lambda e, wq=wq: e.tensor_tensor(
                                out=pbuf[:, 0:wq], in0=pbuf[:, 0:wq],
                                in1=pind[:, 0:wq], op=ALU.mult))
                            for lci in range(lci_min, 4):
                                kc_last = lci * 8 + 7
                                ch.te(lambda e, lci=lci, lci_min=lci_min, kc=kc,
                                      la=(kc == kc_last): e.matmul(
                                    pov[lci],
                                    pbuf[:, (lci - lci_min) * 128:(lci - lci_min + 1) * 128],
                                    vtt[:, kc * 129:(kc + 1) * 129],
                                    start=(kc == 0), stop=la),
                                    inc=(1 if (lci == 3 or kc == kc_last) else 0))
                        for lci in range(4):
                            g4 = bb * 4 + lci
                            tb_ = g4 // (TBS // 128)
                            tto = (g4 % (TBS // 128)) * 128
                            ch.v(lambda e, lci=lci: e.reciprocal(rs_[:, :], pov[lci][:, 128:129]))
                            ch.v(lambda e, lci=lci: e.tensor_scalar(
                                out=obf[:, 0:128], in0=pov[lci][:, 0:128],
                                scalar1=rs_[:, :], scalar2=None, op0=ALU.mult))
                            ch.te(lambda e: e.transpose(ptrb[:, :], obf[:, 0:128], identb[:, :]))
                            ch.v(lambda e: e.tensor_copy(out=pbuf[:, 0:128], in_=ptrb[:, :]))
                            ch.dma(lambda e, h=h, tb_=tb_, tto=tto: e.dma_start(
                                out=ao_scr[:, (tb_ * 32 + h) * TBS + tto:
                                           (tb_ * 32 + h) * TBS + tto + 128],
                                in_=pbuf[:, 0:128]))
            wm[0] = wm[1] = None
            for tb in range(NTBS):
                ch.dma(lambda e, tb=tb: e.dma_start(
                    out=aTs[:, :], in_=ao_scr[:, tb * 32 * TBS:(tb + 1) * 32 * TBS]))
                for jc in range(32):
                    emit_proj_chunk(ow, jc, bitnet, rhsbuf=aTs)
                    if bitnet:
                        ch.v(lambda e, jc=jc, cc=48 + jc: e.tensor_scalar(
                            out=fbuf[:, jc * TBS:(jc + 1) * TBS], in0=pmm[:, 0:TBS],
                            scalar1=svec[:, cc:cc + 1], scalar2=None, op0=ALU.mult))
                    else:
                        ch.v(lambda e, jc=jc: e.tensor_copy(
                            out=fbuf[:, jc * TBS:(jc + 1) * TBS], in_=pmm[:, 0:TBS]))
                emit_tail(tb)
            wff = win["WT_W0"] if L == 0 else win["WT_W1"]
            at_ap = win["AT0"] if L == 0 else win["AT1"]
            b2_ap = win["B20"] if L == 0 else win["B21"]
            NER = 256 if L == 0 else 128
            NERC = NER // 128
            bias_col0 = 80 if L == 0 else 112
            for tb in range(NTBS):
                emit_ln(tb)
                if L == 1:
                    ch.dma(lambda e: e.dma_start(out=m2_s[:, :], in_=win["M2"]))
                    for tt in range(TBS // 128):
                        for dc in range(32):
                            ch.te(lambda e, dc=dc, tt=tt: e.matmul(
                                pbc[0:128, 0:8],
                                nbuf[:, dc * TBS + tt * 128: dc * TBS + tt * 128 + 128],
                                gw_s[:, dc * 8:(dc + 1) * 8],
                                start=(dc == 0), stop=(dc == 31)), inc=(1 if dc == 31 else 0))
                        ch.v(lambda e: e.tensor_copy(out=gl[:, :], in_=pbc[0:128, 0:8]))
                        ch.v(lambda e: e.max(out=top8[:, :], in_=gl[:, :]))
                        ch.v(lambda e: e.tensor_sub(g1[:, :], top8[:, 1:2], top8[:, 0:1]))
                        ch.sc(lambda e: e.activation(g2[:, :], g1[:, :], AF.Exp))
                        ch.v(lambda e: e.tensor_scalar(out=g1[:, :], in0=g2[:, :], scalar1=1.0,
                                                       scalar2=None, op0=ALU.add))
                        ch.v(lambda e: e.reciprocal(g1[:, :], g1[:, :]))
                        ch.v(lambda e: e.tensor_mul(g2[:, :], g2[:, :], g1[:, :]))
                        ch.v(lambda e: e.tensor_scalar(out=wfull[:, :], in0=gl[:, :],
                                                       scalar1=top8[:, 0:1], scalar2=g1[:, :],
                                                       op0=ALU.is_equal, op1=ALU.mult))
                        ch.v(lambda e: e.tensor_scalar(out=wf2[:, :], in0=gl[:, :],
                                                       scalar1=top8[:, 1:2], scalar2=g2[:, :],
                                                       op0=ALU.is_equal, op1=ALU.mult))
                        ch.v(lambda e: e.tensor_add(wfull[:, :], wfull[:, :], wf2[:, :]))
                        ch.te(lambda e: e.transpose(ptrf[0:8, 0:128], wfull[:, :], identf[:, :]))
                        ch.v(lambda e: e.tensor_copy(out=wT_s[:, :], in_=ptrf[0:8, 0:128]))
                        ch.te(lambda e: e.matmul(pqkB[0:128, 0:128], m2_s[:, :], wT_s[:, :],
                                                 start=True, stop=True))
                        ch.v(lambda e, tt=tt: e.tensor_copy(
                            out=we_s[:, tt * 128:(tt + 1) * 128], in_=pqkB[0:128, 0:128]))
                for erc in range(NERC):
                    for h in range(2):
                        ch.dma(lambda e, at_ap=at_ap, erc=erc, h=h: e.dma_start(
                            out=wtile[:, h * 2048:(h + 1) * 2048],
                            in_=at_ap[erc * 128:(erc + 1) * 128, h * 2048:(h + 1) * 2048]),
                            q="gpsimd", wait=wm[h])
                        for dc in range(16 * h, 16 * h + 16):
                            ch.te(lambda e, dc=dc: e.matmul(
                                pqkA[:, 0:TBS], wtile[:, dc * 128:(dc + 1) * 128],
                                nbuf[:, dc * TBS:(dc + 1) * TBS],
                                start=(dc == 0), stop=(dc == 31)),
                                inc=(1 if dc in (15, 31) else 0))
                        wm[h] = ch.mark()
                    if L == 0:
                        ch.v(lambda e, erc=erc: e.tensor_scalar(
                            out=xa_s[:, erc * TBS:(erc + 1) * TBS], in0=pqkA[:, 0:TBS],
                            scalar1=wex[:, erc:erc + 1], scalar2=None, op0=ALU.mult))
                    else:
                        ch.v(lambda e, erc=erc: e.tensor_tensor(
                            out=xa_s[:, erc * TBS:(erc + 1) * TBS], in0=pqkA[:, 0:TBS],
                            in1=we_s[:, 0:TBS], op=ALU.mult))
                for jc in range(32):
                    slot = jc % 2
                    ch.dma(lambda e, jc=jc, b2_ap=b2_ap, slot=slot, NERC=NERC: e.dma_start(
                        out=strm2[:, slot * 256: slot * 256 + NERC * 128],
                        in_=b2_ap[jc * 128:(jc + 1) * 128, 0:NERC * 128]),
                        q="gpsimd", wait=b2_m[slot])
                    emit_proj_chunk(wff, jc, False, close=False)
                    for erc in range(NERC):
                        ch.te(lambda e, erc=erc, slot=slot: e.matmul(
                            pmm[:, 0:TBS],
                            strm2[:, slot * 256 + erc * 128: slot * 256 + (erc + 1) * 128],
                            xa_s[:, erc * TBS:(erc + 1) * TBS],
                            start=False, stop=(erc == NERC - 1)), inc=1)
                    b2_m[slot] = ch.mark()
                    ch.v(lambda e, jc=jc, cc=bias_col0 + jc: e.tensor_scalar(
                        out=fbuf[:, jc * TBS:(jc + 1) * TBS], in0=pmm[:, 0:TBS],
                        scalar1=vecs[:, cc:cc + 1], scalar2=None, op0=ALU.add))
                emit_tail(tb)
        for i in range(4):
            ch.dma(lambda e, i=i: e.dma_start(
                out=xo_out.rearrange("(c p) t -> p c t", p=128)[:, i * 8:(i + 1) * 8, :],
                in_=x_res[:, i * 8192:(i + 1) * 8192]))
        ch.emit(nc, block, semc, sems, semg, sema)
    return nc


_FUSED_NC = None
_FUSED_CONST = None


def _fused_consts():
    global _FUSED_CONST
    if _FUSED_CONST is not None:
        return _FUSED_CONST
    idx = []
    pos = []
    for c in range(NC):
        ids = []
        ps = []
        for bi in range(B):
            for g in _chunks_of(c):
                ids.extend(range(bi * S + g * 128, bi * S + g * 128 + 128))
                ps.extend(range(g * 128, g * 128 + 128))
        idx.append(np.array(ids, np.int64))
        pos.append(np.array(ps, np.int64))
    i = np.arange(0, 128, 2, dtype=np.float64) / 128.0
    inv32 = (1.0 / (10000.0 ** i)).astype(np.float32)
    t32 = np.arange(S, dtype=np.float32)
    freqs32 = t32[:, None] * inv32[None, :]
    cosf = np.cos(freqs32.astype(np.float64)).astype(np.float32)
    sinf = np.sin(freqs32.astype(np.float64)).astype(np.float32)
    cos_sd = np.concatenate([cosf, cosf], axis=1).T
    sin_sd = np.concatenate([sinf, sinf], axis=1).T
    mqk_l = []
    mkq_l = []
    p_ = np.arange(128)
    for c in range(NC):
        ch_ = _chunks_of(c)
        mq = np.zeros((128, 4096), np.float32)
        for lci in range(4):
            g = ch_[lci]
            k_ = np.arange(1024 * lci, 1024 * (lci + 1))[None, :]
            mq[:, 1024 * lci:1024 * (lci + 1)] = np.where(
                k_ <= g * 128 + p_[:, None], 0.0, NEG)
        mqk_l.append(mq.astype(BF))
        mk = np.zeros((128, 4 * 8 * 128), np.float32)
        for lci in range(4):
            g = ch_[lci]
            for kk in range(8):
                kc = lci * 8 + kk
                q_ = np.arange(lci * 128, lci * 128 + 128)[None, :] - lci * 128
                valid = (kc * 128 + p_[:, None]) <= (g * 128 + q_)
                mk[:, (lci * 8 + kk) * 128:(lci * 8 + kk) * 128 + 128] = \
                    np.where(valid, 0.0, NEG)
        mkq_l.append(mk.astype(BF))
    _FUSED_CONST = (idx, pos, cos_sd, sin_sd, mqk_l, mkq_l)
    return _FUSED_CONST


def _tile_w(W):
    W = np.asarray(W, np.float32)
    od = W.shape[0]
    t = W.reshape(od // 128, 128, 32, 128).transpose(0, 3, 2, 1)
    return _f32(t.reshape(od, 4096))


def _tile_b2(B2):
    B2 = np.asarray(B2, np.float32)
    ner = B2.shape[0]
    t = B2.reshape(ner // 128, 128, 32, 128).transpose(2, 1, 0, 3)
    return _f32(t.reshape(4096, ner))


def kernel(L0_attn_hidden_states, L0_attn_q_weight, L0_attn_q_scale,
           L0_attn_k_weight, L0_attn_k_scale, L0_attn_v_weight, L0_attn_v_scale,
           L0_attn_o_weight, L0_attn_o_scale, L0_ffn_W, L0_ffn_bias, L0_ffn_A,
           L0_ffn_Bf, L0_ffn_coeffs, L0_ffn_scales, L1_attn_Wq, L1_attn_Wk,
           L1_attn_Wv, L1_attn_Wo, L1_ffn_gate_w, L1_ffn_base_w, L1_ffn_base_b,
           L1_ffn_lora_A, L1_ffn_lora_B):
    global _FUSED_NC
    x = _f32(np.asarray(L0_attn_hidden_states).reshape(T, D))
    idx, pos, cos_sd, sin_sd, mqk_l, mkq_l = _fused_consts()
    if _FUSED_NC is None:
        _FUSED_NC = build_fused()
    nc = _FUSED_NC
    tr = lambda a: _f32(np.ascontiguousarray(np.asarray(a, np.float32).T))
    colv = lambda a, n: _f32(np.ascontiguousarray(np.asarray(a, np.float32).reshape(n, 128).T))
    vecs = np.zeros((128, 160), np.float32)
    vecs[:, 0:32] = colv(L0_attn_q_scale, 32)
    vecs[:, 32:40] = colv(L0_attn_k_scale, 8)
    vecs[:, 40:48] = colv(L0_attn_v_scale, 8)
    vecs[:, 48:80] = colv(L0_attn_o_scale, 32)
    vecs[:, 80:112] = colv(L0_ffn_bias, 32)
    vecs[:, 112:144] = colv(L1_ffn_base_b, 32)
    M1 = np.zeros((16, 256), np.float32)
    for e_ in range(16):
        M1[e_, e_ * 16:(e_ + 1) * 16] = 1.0
    M2 = np.zeros((8, 128), np.float32)
    for e_ in range(8):
        M2[e_, e_ * 16:(e_ + 1) * 16] = 2.0
    shared = {
        "vecs": vecs, "identbf": IDENT_BF, "identf": IDENT_F32,
        "WT_q0": _tile_w(L0_attn_q_weight), "WT_k0": _tile_w(L0_attn_k_weight),
        "WT_v0": _tile_w(L0_attn_v_weight), "WT_o0": _tile_w(L0_attn_o_weight),
        "Wraw_q0": _f32(L0_attn_q_weight), "Wraw_k0": _f32(L0_attn_k_weight),
        "Wraw_v0": _f32(L0_attn_v_weight), "Wraw_o0": _f32(L0_attn_o_weight),
        "WT_W0": _tile_w(L0_ffn_W),
        "AT0": _tile_w(np.asarray(L0_ffn_A, np.float32).reshape(256, D)),
        "B20": _tile_b2(np.asarray(L0_ffn_Bf, np.float32).transpose(0, 2, 1).reshape(256, D)),
        "coeffs": _f32(np.asarray(L0_ffn_coeffs).reshape(16, 1)),
        "scales16": _f32(np.asarray(L0_ffn_scales).reshape(16, 1)),
        "M1": M1, "M2": M2,
        "WT_q1": _tile_w(L1_attn_Wq), "WT_k1": _tile_w(L1_attn_Wk),
        "WT_v1": _tile_w(L1_attn_Wv), "WT_o1": _tile_w(L1_attn_Wo),
        "WT_W1": _tile_w(L1_ffn_base_w),
        "AT1": _tile_w(np.asarray(L1_ffn_lora_A, np.float32).reshape(128, D)),
        "B21": _tile_b2(np.asarray(L1_ffn_lora_B, np.float32).transpose(0, 2, 1).reshape(128, D)),
        "gwT": tr(L1_ffn_gate_w),
    }
    in_maps = []
    for c in range(NC):
        m = dict(shared)
        m["xT"] = _f32(np.ascontiguousarray(x[idx[c]].T))
        m["cos_t"] = _f32(np.ascontiguousarray(cos_sd[:, pos[c]]))
        m["sin_t"] = _f32(np.ascontiguousarray(sin_sd[:, pos[c]]))
        m["mqk"] = mqk_l[c]
        m["mkq"] = mkq_l[c]
        in_maps.append(m)
    res = run_bass_kernel_spmd(nc, in_maps, core_ids=list(range(NC))).results
    out = np.empty((T, D), np.float32)
    for c in range(NC):
        out[idx[c]] = np.asarray(res[c]["xoT"], np.float32).T
    return out.reshape(B, S, D)
